# version 10
# baseline (speedup 1.0000x reference)
import math
import numpy as np
import ml_dtypes
import concourse.bass as bass
import concourse.mybir as mybir
from concourse.bass_utils import run_bass_kernel_spmd

F32 = mybir.dt.float32
BF16 = mybir.dt.bfloat16
I32 = mybir.dt.int32
AF = mybir.ActivationFunctionType
ALU = mybir.AluOpType
AX = mybir.AxisListType

NEGV = -1.0e30
EPS = 1e-5


def ts(i, n=128):
    return slice(i * n, (i + 1) * n)


class Cfg:
    def __init__(self, d=4096, seq=2048, sub_b=True, sub_c=True,
                 b_stop="full", dbg=False, attn_iters=None):
        self.b_stop = b_stop
        self.dbg = dbg
        self.attn_iters = attn_iters
        self.d = d
        self.seq = seq
        self.kvd = d // 4
        self.heads = d // 128
        self.kvheads = self.kvd // 128
        self.words = d // 32
        self.nblk = d // 128
        self.nch = d // 128
        self.qt = seq // 128
        self.own = seq // 2
        self.nog = d // 512
        self.sub_b = sub_b
        self.sub_c = sub_c
        assert d % 512 == 0 and seq % 1024 == 0


class B:
    ENGS = ("sync", "scalar", "vector", "tensor", "gpsimd")

    def __init__(self, nc):
        self.nc = nc
        self.streams = {k: [] for k in self.ENGS}
        self.sems = {}
        self.counts = {}
        self.waited = {k: {} for k in self.ENGS}
        self.pending = {k: [] for k in self.ENGS}
        self.lastdma = {}
        self.semfactory = None

    def sem(self, name):
        self.counts[name] = 0

    def barrier(self, toks):
        toks = [t for t in toks if t is not None]
        for k in self.ENGS:
            self.pending[k] = list(toks)

    def emit(self, eng, fn, deps=(), inc=None, dinc=None):
        deps = list(deps) + self.pending[eng]
        self.pending[eng] = []
        waits = []
        for t in deps:
            if t is None:
                continue
            s, v = t
            if v <= 0:
                continue
            w = self.waited[eng]
            if w.get(s, -1) >= v:
                continue
            w[s] = v
            waits.append((s, v))
        out_tok = None
        inc_s, amt = None, 0
        if inc is not None:
            self.counts[inc] += 1
            out_tok = (inc, self.counts[inc])
            inc_s, amt = inc, 1
        elif dinc is not None:
            self.counts[dinc] += 16
            out_tok = (dinc, self.counts[dinc])
            inc_s, amt = dinc, 16
        self.streams[eng].append((fn, waits, inc_s, amt))
        return out_tok

    def dma(self, queue, out, in_, deps=(), stream="misc", **kw):
        sem = "d_" + stream
        if sem not in self.counts:
            self.counts[sem] = 0
            self.sems[sem] = self.semfactory(sem)
            self.lastdma[sem] = None
        deps = list(deps) + [self.lastdma[sem]]
        tok = self.emit(queue, lambda e: e.dma_start(out=out, in_=in_, **kw),
                        deps=deps, dinc=sem)
        self.lastdma[sem] = tok
        return tok

    def build(self, block):
        semobjs = self.sems

        def run(eng_name, engine):
            for (fn, waits, inc_s, amt) in self.streams[eng_name]:
                for (s, v) in waits:
                    engine.wait_ge(semobjs[s], v)
                ins = fn(engine)
                if inc_s is not None:
                    ins.then_inc(semobjs[inc_s], amt)

        @block.sync
        def _(e):
            run("sync", e)

        @block.scalar
        def _(e):
            run("scalar", e)

        @block.vector
        def _(e):
            run("vector", e)

        @block.tensor
        def _(e):
            run("tensor", e)

        @block.gpsimd
        def _(e):
            run("gpsimd", e)


def make_tables(cfg, half):
    t = {}
    t["mask32"] = np.broadcast_to((np.int32(1) << np.arange(32, dtype=np.int32)),
                                  (128, 32)).copy()
    t["ident"] = np.eye(128, dtype=ml_dtypes.bfloat16)
    invf = 1.0 / (10000.0 ** (np.arange(0, 128, 2, dtype=np.float64) / 128.0))
    invf_ext = np.concatenate([invf, invf])
    u = np.arange(128)
    pb = (u + 64 * half) % 128
    blocks = np.arange(cfg.seq // 128)
    seq_of_stored = (128 * blocks[:, None] + pb[None, :]).reshape(-1)
    ang = invf_ext[:, None] * seq_of_stored[None, :].astype(np.float64)
    t["cos_k"] = np.cos(ang).astype(np.float32)
    t["sin_k"] = np.sin(ang).astype(np.float32)
    o = np.arange(cfg.own)
    own_seq = 128 * (o // 64) + 64 * half + (o % 64)
    ang_q = invf_ext[:, None] * own_seq[None, :].astype(np.float64)
    sc = 1.0 / math.sqrt(128.0)
    t["cos_q"] = (np.cos(ang_q) * sc).astype(np.float32)
    t["sin_q"] = (np.sin(ang_q) * sc).astype(np.float32)
    r = np.arange(64)
    qpos_in_blk = 64 * half + r
    m = np.where(pb[None, :] <= qpos_in_blk[:, None], 0.0, NEGV).astype(np.float32)
    t["m_end"] = np.concatenate([m, m], axis=0)
    mr = np.where(pb[None, :] > qpos_in_blk[:, None], 0.0, NEGV).astype(np.float32)
    t["m_rec"] = np.concatenate([mr, mr], axis=0)
    return t


def permute_seq(x, half):
    if half == 0:
        return np.ascontiguousarray(x)
    S, d = x.shape
    v = x.reshape(S // 128, 2, 64, d)
    return np.ascontiguousarray(v[:, ::-1].reshape(S, d))


def build_nc(cfg):
    nc = bass.Bass("TRN2", target_bir_lowering=False)
    d, S = cfg.d, cfg.seq
    nch, nog, kvh = cfg.nch, cfg.nog, cfg.kvheads

    xin = nc.dram_tensor("xin", [S, d], F32, kind="ExternalInput").ap()
    qw4 = nc.dram_tensor("qw4", [4, d, cfg.words], I32, kind="ExternalInput").ap()
    lut4 = nc.dram_tensor("lut4", [d, 16], F32, kind="ExternalInput").ap()
    biasb = nc.dram_tensor("biasb", [128, d], F32, kind="ExternalInput").ap()
    wq_t = nc.dram_tensor("wq_t", [d, d], F32, kind="ExternalInput").ap()
    wk_t = nc.dram_tensor("wk_t", [d, cfg.kvd], F32, kind="ExternalInput").ap()
    wv_t = nc.dram_tensor("wv_t", [d, cfg.kvd], F32, kind="ExternalInput").ap()
    wo_t = nc.dram_tensor("wo_t", [d, d], F32, kind="ExternalInput").ap()
    mask32 = nc.dram_tensor("mask32", [128, 32], I32, kind="ExternalInput").ap()
    ident_in = nc.dram_tensor("ident", [128, 128], BF16, kind="ExternalInput").ap()
    cos_k = nc.dram_tensor("cos_k", [128, S], F32, kind="ExternalInput").ap()
    sin_k = nc.dram_tensor("sin_k", [128, S], F32, kind="ExternalInput").ap()
    cos_q = nc.dram_tensor("cos_q", [128, cfg.own], F32, kind="ExternalInput").ap()
    sin_q = nc.dram_tensor("sin_q", [128, cfg.own], F32, kind="ExternalInput").ap()
    m_end = nc.dram_tensor("m_end", [128, 128], F32, kind="ExternalInput").ap()
    m_rec = nc.dram_tensor("m_rec", [128, 128], F32, kind="ExternalInput").ap()

    xout = nc.dram_tensor("xout", [cfg.own, d], F32, kind="ExternalOutput").ap()

    wlt = nc.dram_tensor("wlt", [cfg.nblk, d, 128], BF16).ap()
    lnTsp = nc.dram_tensor("lnTsp", [S // 512, 128, nch * 512], BF16).ap()
    xres = nc.dram_tensor("xres", [S, d], F32).ap()
    qsp = nc.dram_tensor("qsp", [nch, 128, cfg.own], BF16).ap()
    asp = nc.dram_tensor("asp", [nch, 128, cfg.own], BF16).ap()

    def own_rows(src, i, cols=None):
        v = src.rearrange("(a r) d -> a r d", r=64)
        if cols is not None:
            v = v[:, :, cols]
        return [v[4 * i + 2 * a] for a in range(2)]

    def dma_own(queue, sbuf_tile, src, i, cols=None, write=False, deps=(),
                stream="misc"):
        g0, g1 = own_rows(src, i, cols)
        if write:
            b.dma(queue, g0, sbuf_tile[0:64, :], deps=deps, stream=stream)
            return b.dma(queue, g1, sbuf_tile[64:128, :], deps=deps, stream=stream)
        b.dma(queue, sbuf_tile[0:64, :], g0, deps=deps, stream=stream)
        return b.dma(queue, sbuf_tile[64:128, :], g1, deps=deps, stream=stream)

    b = B(nc)
    for s in ("s_pe", "s_dv", "s_ac", "s_pl"):
        b.sem(s)

    from contextlib import ExitStack
    with ExitStack() as root:
        def mk(stack, name, shape, dt):
            return stack.enter_context(nc.sbuf_tensor(name, shape, dt))

        def mkp(stack, name, shape, dt=F32):
            return stack.enter_context(nc.psum_tensor(name, shape, dt))

        for s in list(b.counts):
            b.sems[s] = root.enter_context(nc.semaphore(s))
        b.semfactory = lambda name: root.enter_context(nc.semaphore(name))
        block = root.enter_context(nc.Block())

        mask32_s = mk(root, "mask32_s", [128, 32], I32)
        ident_s = mk(root, "ident_s", [128, 128], BF16)
        stat_s = mk(root, "stat_s", [128, 64], F32)
        mv_s = mk(root, "mv_s", [128, 8], F32)
        eps_s = mk(root, "eps_s", [128, 1], F32)

        c_toks = [b.dma("sync", mask32_s[:], mask32, stream="c0"),
                  b.dma("sync", ident_s[:], ident_in, stream="c0"),
                  b.emit("vector", lambda e: e.memset(eps_s[:], EPS), inc="s_dv")]
        consts = tuple(c_toks)

        def ln_into(xt_tile, ld_tok, ln_s, ln_ps, lnT_view, col, deps=()):
            nst = d // 512
            st_toks = []
            for k in range(nst):
                st_toks.append(b.emit("vector", lambda e, k=k: e.bn_stats(
                    out=stat_s[:, 6 * k:6 * (k + 1)], in_=xt_tile[:, ts(k, 512)]),
                    deps=[ld_tok] + list(deps), inc="s_dv"))
            ag = b.emit("vector", lambda e, n=nst: e.bn_aggr(
                out=mv_s[:, 0:2], in_=stat_s[:, 0:6 * n]), deps=st_toks, inc="s_dv")
            l1 = b.emit("scalar", lambda e: e.activation(
                out=mv_s[:, 2:3], in_=mv_s[:, 1:2], func=AF.Ln, bias=eps_s[:]),
                deps=[ag] + list(consts), inc="s_ac")
            l2 = b.emit("scalar", lambda e: e.activation(
                out=mv_s[:, 3:4], in_=mv_s[:, 2:3], func=AF.Exp, scale=-0.5),
                deps=[l1], inc="s_ac")
            ap_tok = b.emit("vector", lambda e: e.tensor_scalar(
                out=ln_s[:], in0=xt_tile[:], scalar1=mv_s[:, 0:1],
                scalar2=mv_s[:, 3:4], op0=ALU.subtract, op1=ALU.mult),
                deps=[l2], inc="s_dv")
            evh = []
            for ch in range(nch):
                dep_t = [ap_tok] + list(consts)
                if len(evh) >= 2:
                    dep_t.append(evh[-2])
                pe_tok = b.emit("tensor", lambda e, ch=ch: e.transpose(
                    ln_ps[ch % 2][:, 0:128], ln_s[:, ts(ch)], ident_s[:]),
                    deps=dep_t, inc="s_pe")
                evh.append(b.emit("scalar", lambda e, ch=ch: e.copy(
                    out=lnT_view(ch, col), in_=ln_ps[ch % 2][:, 0:128]),
                    deps=[pe_tok], inc="s_ac"))
            return evh[-1]

        deq_blk = []
        with ExitStack() as sa:
            qw_s2 = [mk(sa, f"qw_s{i}", [128, 4 * cfg.words], I32) for i in range(2)]
            lut_s2 = [mk(sa, f"lut_s{i}", [128, 16], F32) for i in range(2)]
            idx_s2 = [mk(sa, f"idx_s{i}", [128, d], BF16) for i in range(2)]
            sgn_s2 = [mk(sa, f"sgn_s{i}", [128, d], BF16) for i in range(2)]
            and2 = [mk(sa, f"and_s{i}", [128, d // 2], I32) for i in range(2)]
            acc_s = mk(sa, "acc_s", [128, d], BF16)
            wtst_s = mk(sa, "wtst_s", [128, d], BF16)
            tp_ps2 = [mkp(sa, f"tp_ps{i}", [128, 1024], BF16) for i in range(2)]

            mask_view2 = mask32_s.ap().unsqueeze(1).broadcast_to(
                [128, cfg.words // 2, 32])

            dstate = dict(last_tp=None, sgn_by_buf={0: None, 1: None})
            prev_sel = [None, None]
            prev_idx = [[], []]

            def emit_deq(blk):
                par = blk % 2
                qws, luts, idxs, sgns = (qw_s2[par], lut_s2[par],
                                         idx_s2[par], sgn_s2[par])
                qtok = b.dma("scalar",
                             qws.ap().rearrange("p (a w) -> p a w", a=4),
                             qw4[:, ts(blk), :].transpose([1, 0, 2]),
                             deps=list(consts) + prev_idx[par],
                             stream=f"qw{par}")
                ltok = b.dma("scalar", luts[:], lut4[ts(blk), :],
                             deps=list(consts) + [prev_sel[par]],
                             stream=f"lut{par}")
                hwd = d // 2
                hwords = cfg.words // 2
                idx_half = [None, None]
                for p in range(4):
                    for h in range(2):
                        s_i = (2 * p + h) % 2
                        ab = and2[s_i]
                        w0 = p * cfg.words + h * hwords
                        qv = qws.ap()[:, w0:w0 + hwords].unsqueeze(2).broadcast_to(
                            [128, hwords, 32])
                        a_tok = b.emit("vector", lambda e, qv=qv, ab=ab: e.tensor_tensor(
                            out=ab.ap().rearrange("p (w t) -> p w t", t=32),
                            in0=qv, in1=mask_view2, op=ALU.bitwise_and),
                            deps=[qtok, dstate["sgn_by_buf"][s_i],
                                  idx_half[h]] + list(consts),
                            inc="s_dv")
                        dst_t = idxs if p == 0 else sgns
                        s_tok = b.emit("scalar", lambda e, ab=ab, dst_t=dst_t, h=h: e.activation(
                            out=dst_t[:, h * hwd:(h + 1) * hwd], in_=ab[:],
                            func=AF.Sign),
                            deps=[a_tok, prev_sel[par]], inc="s_ac")
                        dstate["sgn_by_buf"][s_i] = s_tok
                        if p == 0:
                            idx_half[h] = s_tok
                        else:
                            idx_half[h] = b.emit("vector", lambda e, p=p, h=h, idxs=idxs, sgns=sgns: e.scalar_tensor_tensor(
                                out=idxs[:, h * hwd:(h + 1) * hwd],
                                in0=sgns[:, h * hwd:(h + 1) * hwd],
                                scalar=float(1 << p),
                                in1=idxs[:, h * hwd:(h + 1) * hwd],
                                op0=ALU.mult, op1=ALU.add),
                                deps=[s_tok, idx_half[h]], inc="s_dv")
                prev_idx[par] = list(idx_half)
                idx_tok = idx_half
                acc_tok = b.emit("vector", lambda e, idxs=idxs, luts=luts: e.tensor_scalar(
                    out=acc_s[:], in0=idxs[:], scalar1=0.0,
                    scalar2=luts[:, 0:1], op0=ALU.is_equal, op1=ALU.mult),
                    deps=idx_tok + [ltok, dstate["last_tp"]], inc="s_dv")
                for c in range(1, 16):
                    t_tok = b.emit("vector", lambda e, c=c, idxs=idxs, luts=luts, sgns=sgns: e.tensor_scalar(
                        out=sgns[:], in0=idxs[:],
                        scalar1=float(c), scalar2=luts[:, c:c + 1],
                        op0=ALU.is_equal, op1=ALU.mult),
                        deps=idx_tok + [ltok, acc_tok], inc="s_dv")
                    acc_tok = b.emit("vector", lambda e, sgns=sgns: e.tensor_tensor(
                        out=acc_s[:], in0=acc_s[:], in1=sgns[:], op=ALU.add),
                        deps=[t_tok], inc="s_dv")
                prev_sel[par] = acc_tok
                evt_hist = []
                for ch in range(nch):
                    dep_t = [acc_tok] + list(consts)
                    if ch == 0 and deq_blk:
                        dep_t.append(deq_blk[-1])
                    if len(evt_hist) >= 2:
                        dep_t.append(evt_hist[-2])
                    pe_tok = b.emit("tensor", lambda e, ch=ch: e.transpose(
                        tp_ps2[ch % 2][:, 0:128], acc_s[:, ts(ch)], ident_s[:]),
                        deps=dep_t, inc="s_pe")
                    evt_hist.append(b.emit("scalar", lambda e, ch=ch: e.copy(
                        out=wtst_s[:, ts(ch)], in_=tp_ps2[ch % 2][:, 0:128]),
                        deps=[pe_tok], inc="s_ac"))
                dstate["last_tp"] = pe_tok
                st = b.dma("scalar",
                           wlt[blk].rearrange("(c p) f -> c p f", p=128).transpose([1, 0, 2]),
                           wtst_s.ap().rearrange("p (c f) -> p c f", f=128),
                           deps=[evt_hist[-1]], stream="dst")
                deq_blk.append(st)

            ffn_p0, ffn_p1 = _ffn_split(
                cfg, b, nc, sa, mk, mkp, ln_into, consts,
                biasb=biasb, wlt=wlt, lnTsp=lnTsp,
                src=lambda i: xin[ts(i), :],
                dst=lambda i: xres[ts(i), :], ntiles=S // 128)

            H = cfg.nblk // 2
            for blk in range(H):
                emit_deq(blk)
            rest = list(range(H, cfg.nblk))
            n_slots = (S // 512) * (cfg.nog // 2)
            di = [0]

            def after_og(slot):
                want = min((slot + 1) * len(rest) // n_slots, len(rest))
                while di[0] < want:
                    emit_deq(rest[di[0]])
                    di[0] += 1
            ffn_p0(deq_blk, after_og)
            while di[0] < len(rest):
                emit_deq(rest[di[0]])
                di[0] += 1
            ffn_toks = ffn_p1(deq_blk)
        b.barrier(ffn_toks + [deq_blk[-1]])

        final_deps = ffn_toks

        if cfg.sub_b:
            _sublayer_b(cfg, b, nc, mk, mkp, ln_into, consts,
                        xres=xres, qsp=qsp, asp=asp, own_rows=own_rows,
                        wq_t=wq_t, wk_t=wk_t, wv_t=wv_t, wo_t=wo_t,
                        cos_q=cos_q, sin_q=sin_q, cos_k=cos_k, sin_k=sin_k,
                        m_end=m_end, m_rec=m_rec, ident_s=ident_s,
                        final_deps=final_deps)
            final_deps = []

        if cfg.sub_c:
            with ExitStack() as sc:
                ffn2 = _ffn_sublayer(
                    cfg, b, nc, sc, mk, mkp, ln_into, consts,
                    biasb=biasb, wlt=wlt, deq_blk=deq_blk,
                    src_tile=lambda i: own_rows(xres, i),
                    dst_tile=lambda i: xout[ts(i), :],
                    ntiles=cfg.own // 128, pre="c")
            final_deps = ffn2
        else:
            cp = []
            if True:
                cp_t = mk(root, "cp_t", [128, d], F32)
                for i in range(cfg.own // 128):
                    t1 = dma_own("sync", cp_t, xres, i,
                                 deps=final_deps + cp[-1:], stream="cp")
                    cp.append(b.dma("scalar", xout[ts(i), :], cp_t[:], deps=[t1],
                                    stream="cps"))
            final_deps = cp

        b.emit("scalar", lambda e: e.nop(), deps=final_deps)

        b.build(block)
    return nc


def _sublayer_b(cfg, b, nc, mk, mkp, ln_into, consts, *, xres, qsp, asp,
                own_rows, wq_t, wk_t, wv_t, wo_t, cos_q, sin_q, cos_k, sin_k,
                m_end, m_rec, ident_s, final_deps):
    from contextlib import ExitStack
    d, S, nch, nog, kvh = cfg.d, cfg.seq, cfg.nch, cfg.nog, cfg.kvheads
    VW = min(512, cfg.kvd)

    with ExitStack() as skv:
        kbuf = [mk(skv, f"kbuf{g}", [128, S], BF16) for g in range(kvh)]
        vbuf = [mk(skv, f"vbuf{t}", [128, cfg.kvd], BF16) for t in range(S // 128)]
        proj_toks = []
        with ExitStack() as sp:
            xb_t = [mk(sp, f"xb_t{i}", [128, d], F32) for i in range(2)]
            ln_s = mk(sp, "ln2_s", [128, d], BF16)
            ln2T_t = mk(sp, "ln2T", [128, nch * 512], BF16)

            def ln2T(ch):
                return ln2T_t[:, ch * 512:(ch + 1) * 512]
            cosq_s = mk(sp, "cosq_s", [128, cfg.own], F32)
            sinq_s = mk(sp, "sinq_s", [128, cfg.own], F32)
            cosk_s = mk(sp, "cosk_s", [128, S], F32)
            sink_s = mk(sp, "sink_s", [128, S], F32)
            wcol_s = [mk(sp, f"wcol_s{i}", [128, nch * 128], BF16) for i in range(2)]
            wf32 = [mk(sp, f"wf32_{i}", [128, d // 2], F32) for i in range(2)]
            rsc = mk(sp, "rsc", [128, 1024], F32)
            qstg2 = [mk(sp, f"qstg{i}", [128, 256], BF16) for i in range(2)]
            ln_ps = [mkp(sp, f"ln2_ps{i}", [128, 1024], BF16) for i in range(2)]
            pp = [mkp(sp, "pr_ps", [128, 512]), mkp(sp, "pr_ps2", [128, 512])]

            rt = [b.dma("sync", cosq_s[:], cos_q, stream="c0"),
                  b.dma("sync", sinq_s[:], sin_q, stream="c0"),
                  b.dma("sync", cosk_s[:], cos_k, stream="c0"),
                  b.dma("sync", sink_s[:], sin_k, stream="c0")]
            rope_tok = rt[-1]

            def lnT_view(ch, col):
                return ln2T(ch)[:, ts(col)]

            ps_free = [None, None]
            wprev = {}
            wfprev = {}
            qst_prev = {}
            last_rope = [None]
            hc = nch // 2

            def load_wcol(w_ap, oc, par, deps):
                wsb = wcol_s[par]
                if oc % 4 == 3:
                    wtok = b.dma("gpsimd",
                                 wsb.ap().rearrange("p (c f) -> p c f", f=128),
                                 w_ap[:, ts(oc)].rearrange(
                                     "(c p) f -> c p f", p=128).transpose([1, 0, 2]),
                                 deps=deps + [wprev.get(par)], stream="bwg")
                    return [wtok, wtok]
                cts = []
                for hv in range(2):
                    lv = w_ap[hv * (d // 2):(hv + 1) * (d // 2), ts(oc)].rearrange(
                        "(c p) f -> c p f", p=128).transpose([1, 0, 2])
                    lt = b.dma("sync",
                               wf32[par].ap().rearrange("p (c f) -> p c f", f=128),
                               lv, deps=deps + [wfprev.get(par)],
                               stream=f"bw{par}")
                    ct = b.emit("scalar", lambda e, par=par, hv=hv, wsb=wsb: e.copy(
                        out=wsb[:, hv * hc * 128:(hv + 1) * hc * 128],
                        in_=wf32[par][:]),
                        deps=[lt, wprev.get(par)], inc="s_ac")
                    wfprev[par] = ct
                    cts.append(ct)
                return cts
            for hf in range(S // 512):
                lt_done = []
                for i in range(4):
                    tile = hf * 4 + i
                    ld = b.dma("sync", xb_t[i % 2][:], xres[ts(tile), :],
                               deps=final_deps + lt_done[-2:] + proj_toks[-1:],
                               stream=f"bx{i % 2}")
                    lt_done.append(ln_into(xb_t[i % 2], ld, ln_s, ln_ps,
                                           lnT_view, i, deps=lt_done[-1:]))
                lnr = [lt_done[-1]]

                def own_view(ch):
                    v = ln2T(ch).rearrange("p (a g r) -> p a g r", g=2, r=64)
                    return v[:, :, 0]

                own0 = hf * 256
                for oc in range(nch):
                    qstg = qstg2[oc % 2]
                    wsb = wcol_s[oc % 2]
                    cts = load_wcol(wq_t, oc, oc % 2, lnr)
                    pps = pp[oc % 2]
                    mm = None
                    for i in range(nch):
                        mm = b.emit("tensor", lambda e, i=i, wsb=wsb, pps=pps: e.matmul(
                            pps[:, 0:256], wsb[:, ts(i)], own_view(i),
                            start=(i == 0), stop=(i == nch - 1)),
                            deps=[cts[i // hc], ps_free[oc % 2]] + lnr,
                            inc="s_pe" if i == nch - 1 else None)
                    wprev[oc % 2] = mm
                    csl = slice(own0, own0 + 256)
                    r1 = b.emit("vector", lambda e, pps=pps, csl=csl: e.tensor_tensor(
                        out=rsc[0:64, 0:256], in0=pps[0:64, 0:256],
                        in1=cosq_s[0:64, csl], op=ALU.mult),
                        deps=[mm, rope_tok, last_rope[0]], inc="s_dv")
                    r2 = b.emit("vector", lambda e, pps=pps, csl=csl: e.tensor_tensor(
                        out=rsc[0:64, 256:512], in0=pps[64:128, 0:256],
                        in1=sinq_s[0:64, csl], op=ALU.mult),
                        deps=[mm, last_rope[0]], inc="s_dv")
                    r3 = b.emit("vector", lambda e, qstg=qstg: e.tensor_tensor(
                        out=qstg[0:64, 0:256], in0=rsc[0:64, 0:256],
                        in1=rsc[0:64, 256:512], op=ALU.subtract),
                        deps=[r1, r2, qst_prev.get(oc % 2)], inc="s_dv")
                    r4 = b.emit("vector", lambda e, pps=pps, csl=csl: e.tensor_tensor(
                        out=rsc[0:64, 512:768], in0=pps[64:128, 0:256],
                        in1=cosq_s[64:128, csl], op=ALU.mult),
                        deps=[mm, last_rope[0]], inc="s_dv")
                    r5 = b.emit("vector", lambda e, pps=pps, csl=csl: e.tensor_tensor(
                        out=rsc[0:64, 768:1024], in0=pps[0:64, 0:256],
                        in1=sinq_s[64:128, csl], op=ALU.mult),
                        deps=[mm, last_rope[0]], inc="s_dv")
                    r6 = b.emit("vector", lambda e, qstg=qstg: e.tensor_tensor(
                        out=qstg[64:128, 0:256], in0=rsc[0:64, 512:768],
                        in1=rsc[0:64, 768:1024], op=ALU.add),
                        deps=[r4, r5, qst_prev.get(oc % 2)], inc="s_dv")
                    last_rope[0] = r6
                    ps_free[oc % 2] = r5
                    qst = b.dma("scalar", qsp[oc, :, csl], qstg[:], deps=[r3, r6],
                                stream=f"qs{oc % 2}")
                    qst_prev[oc % 2] = qst
                    proj_toks.append(qst)
                c0 = hf * 512
                csl = slice(c0, c0 + 512)
                for oc in range(kvh):
                    wsb = wcol_s[oc % 2]
                    cts = load_wcol(wk_t, oc, oc % 2, lnr)
                    pps = pp[oc % 2]
                    mm = None
                    for i in range(nch):
                        mm = b.emit("tensor", lambda e, i=i, wsb=wsb, pps=pps: e.matmul(
                            pps[:], wsb[:, ts(i)], ln2T(i),
                            start=(i == 0), stop=(i == nch - 1)),
                            deps=[cts[i // hc], ps_free[oc % 2]] + lnr,
                            inc="s_pe" if i == nch - 1 else None)
                    wprev[oc % 2] = mm
                    k1 = b.emit("vector", lambda e, pps=pps, csl=csl: e.tensor_tensor(
                        out=rsc[0:64, 0:512], in0=pps[0:64, :],
                        in1=cosk_s[0:64, csl], op=ALU.mult),
                        deps=[mm, rope_tok, last_rope[0]], inc="s_dv")
                    k2 = b.emit("vector", lambda e, pps=pps, csl=csl: e.tensor_tensor(
                        out=rsc[0:64, 512:1024], in0=pps[64:128, :],
                        in1=sink_s[0:64, csl], op=ALU.mult),
                        deps=[mm, last_rope[0]], inc="s_dv")
                    k3 = b.emit("vector", lambda e, oc=oc, csl=csl: e.tensor_tensor(
                        out=kbuf[oc][0:64, csl], in0=rsc[0:64, 0:512],
                        in1=rsc[0:64, 512:1024], op=ALU.subtract),
                        deps=[k1, k2], inc="s_dv")
                    k4 = b.emit("vector", lambda e, pps=pps, csl=csl: e.tensor_tensor(
                        out=rsc[64:128, 0:512], in0=pps[64:128, :],
                        in1=cosk_s[64:128, csl], op=ALU.mult),
                        deps=[mm, last_rope[0]], inc="s_dv")
                    k5 = b.emit("vector", lambda e, pps=pps, csl=csl: e.tensor_tensor(
                        out=rsc[64:128, 512:1024], in0=pps[0:64, :],
                        in1=sink_s[64:128, csl], op=ALU.mult),
                        deps=[mm, last_rope[0]], inc="s_dv")
                    k6 = b.emit("vector", lambda e, oc=oc, csl=csl: e.tensor_tensor(
                        out=kbuf[oc][64:128, csl], in0=rsc[64:128, 0:512],
                        in1=rsc[64:128, 512:1024], op=ALU.add),
                        deps=[k4, k5], inc="s_dv")
                    last_rope[0] = k6
                    ps_free[oc % 2] = k6
                    proj_toks += [k3, k6]
                with ExitStack() as sv:
                    v_ps = [mkp(sv, f"v_ps{hf}_{m}", [128, VW]) for m in range(4)]
                    bank_last = {m: None for m in range(4)}
                    for og in range(cfg.kvd // VW):
                        mmv = None
                        for i in range(nch):
                            wsb = wcol_s[i % 2]
                            if i % 4 == 3:
                                wtok = b.dma("gpsimd", wsb[:, 0:VW],
                                             wv_t[ts(i), og * VW:(og + 1) * VW],
                                             deps=lnr + [wprev.get(i % 2)],
                                             stream="bwg")
                            else:
                                lt = b.dma("sync", wf32[i % 2][:, 0:VW],
                                           wv_t[ts(i), og * VW:(og + 1) * VW],
                                           deps=lnr + [wfprev.get(i % 2)],
                                           stream=f"bw{i % 2}")
                                wtok = b.emit("scalar", lambda e, i=i, wsb=wsb: e.copy(
                                    out=wsb[:, 0:VW], in_=wf32[i % 2][:, 0:VW]),
                                    deps=[lt, wprev.get(i % 2)], inc="s_ac")
                                wfprev[i % 2] = wtok
                            for m in range(4):
                                dep_m = [wtok] + lnr
                                if i == 0:
                                    dep_m.append(bank_last[m])
                                mmv = b.emit("tensor",
                                             lambda e, i=i, m=m, wsb=wsb: e.matmul(
                                                 v_ps[m][:], ln2T(i)[:, ts(m)],
                                                 wsb[:, 0:VW],
                                                 start=(i == 0), stop=(i == nch - 1)),
                                             deps=dep_m,
                                             inc="s_pe" if m == 3 else None)
                            wprev[i % 2] = mmv
                        for m in range(4):
                            tile = hf * 4 + m
                            tk = b.emit("scalar", lambda e, m=m, og=og, tile=tile: e.copy(
                                out=vbuf[tile][:, og * VW:(og + 1) * VW],
                                in_=v_ps[m][:]), deps=[mmv], inc="s_ac")
                            bank_last[m] = tk
                            proj_toks.append(tk)
        b.barrier(proj_toks)

        attn_toks = []
        with ExitStack() as sat:
            qcur2 = [mk(sat, f"qcur{i}", [128, nch * 64], BF16) for i in range(2)]
            s_sb = [[mk(sat, f"s_sb{gp}_{p}", [128, S], F32) for p in range(2)]
                    for gp in range(2)]
            p_sb = [[mk(sat, f"p_sb{gp}_{p}", [128, S], BF16) for p in range(2)]
                    for gp in range(2)]
            pt_sb = [mk(sat, f"pt_sb{gp}", [128, (S // 128) * 256], BF16)
                     for gp in range(2)]
            cm_sb = [mk(sat, f"cm_sb{gp}", [128, S // 16], F32) for gp in range(2)]
            m8_sb = [mk(sat, f"m8_sb{gp}", [128, 8], F32) for gp in range(2)]
            ta_sb = [mk(sat, f"ta_sb{gp}", [128, S // 16], F32) for gp in range(2)]
            mx_sb = [mk(sat, f"mx_sb{gp}", [128, 128], F32) for gp in range(2)]
            rs_sb = [mk(sat, f"rs_sb{gp}", [128, 8], F32) for gp in range(2)]
            mend_s = mk(sat, "mend_s", [128, 128], F32)
            mrec_s = mk(sat, "mrec_s", [128, 128], F32)
            avst = [mk(sat, f"avst{gp}", [128, 256], BF16) for gp in range(2)]
            sps = [[mkp(sat, f"s_ps{gp}_{p}", [128, 512]) for p in range(2)]
                   for gp in range(2)]
            ptp = [mkp(sat, f"ptp{i}", [128, 1024], BF16) for i in range(2)]
            av_ps = [mkp(sat, f"av_ps{gp}", [128, 256]) for gp in range(2)]

            mtok = [b.dma("sync", mend_s[:], m_end, stream="c0"),
                    b.dma("sync", mrec_s[:], m_rec, stream="c0")]

            iters = [(j, g) for j in range(cfg.qt) for g in range(kvh)]
            if cfg.attn_iters is not None:
                iters = iters[:cfg.attn_iters]
            qload_tok = {}
            last_score_q = {}
            ps_prev = [[None, None], [None, None]]
            sm_prev = [[None, None], [None, None]]
            ptk_prev = [[None, None], [None, None]]
            av_prev = [None, None]
            avc_prev = [None, None]
            evs = [None, None]
            store_prev = None

            for it, (j, g) in enumerate(iters):
                gp = it % 2
                L = 128 * (j + 1)
                nchk = L // 16
                if g == 0:
                    qload_tok[j] = b.dma(
                        "sync", qcur2[j % 2].ap().rearrange("p (c f) -> p c f", f=64),
                        qsp[:, :, ts(j, 64)].transpose([1, 0, 2]),
                        deps=proj_toks[-1:] + [last_score_q.get(j - 2)],
                        stream=f"ql{j % 2}")
                qc = qcur2[j % 2]
                ev_pair = []
                for p in range(2):
                    pps = sps[gp][p]
                    pane = s_sb[gp][p]
                    h0 = 4 * g + 2 * p
                    ev_toks = []
                    dtile = L - 128
                    for c0 in range(0, L, 512):
                        cw = min(512, L - c0)
                        mdep = [qload_tok[j], ps_prev[gp][p]]
                        m1 = b.emit("tensor", lambda e, h0=h0, c0=c0, cw=cw, pps=pps, g=g, qc=qc: e.matmul(
                            pps[0:64, 0:cw], qc[:, ts(h0, 64)],
                            kbuf[g][:, c0:c0 + cw], start=True, stop=True,
                            tile_position=(0, 0)), deps=mdep, inc=None)
                        m2 = b.emit("tensor", lambda e, h0=h0, c0=c0, cw=cw, pps=pps, g=g, qc=qc: e.matmul(
                            pps[64:128, 0:cw], qc[:, ts(h0 + 1, 64)],
                            kbuf[g][:, c0:c0 + cw], start=True, stop=True,
                            tile_position=(0, 64)), deps=mdep, inc="s_pe")
                        last_score_q[j] = m2
                        lo, hi = c0, c0 + cw
                        ev = None
                        if dtile > lo:
                            ev = b.emit("vector", lambda e, lo=lo, m=min(dtile, hi), pps=pps, pane=pane: e.tensor_copy(
                                out=pane[:, lo:m], in_=pps[:, 0:m - lo]),
                                deps=[m2, sm_prev[gp][p]], inc="s_dv")
                        if hi > dtile:
                            ev = b.emit("vector", lambda e, lo=lo, dtile=dtile, hi=hi, pps=pps, pane=pane: e.tensor_tensor(
                                out=pane[:, dtile:hi],
                                in0=pps[:, dtile - lo:hi - lo],
                                in1=mend_s[:, 0:hi - dtile], op=ALU.add),
                                deps=[m2, mtok[0], sm_prev[gp][p]] + ([ev] if ev else []),
                                inc="s_dv")
                        ps_prev[gp][p] = ev
                        ev_toks.append(ev)
                    ev_pair.append(ev_toks)
                nm_pair = []
                for p in range(2):
                    nm, extok = _softmax_pair(cfg, b, j, L, nchk, s_sb[gp][p],
                                              p_sb[gp][p], cm_sb[gp], m8_sb[gp],
                                              ta_sb[gp], mx_sb[gp], rs_sb[gp],
                                              mrec_s, mtok, ev_pair[p],
                                              ptk_prev[gp][p])
                    sm_prev[gp][p] = extok
                    nm_pair.append(nm)
                pair_ptk = []
                for p in range(2):
                    ptk = None
                    for kt in range(j + 1):
                        dep_t = [nm_pair[p], av_prev[gp], evs[kt % 2]] + list(consts)
                        pe_t = b.emit("tensor", lambda e, kt=kt, p=p, gp=gp: e.transpose(
                            ptp[kt % 2][:, 0:128], p_sb[gp][p][:, ts(kt)],
                            ident_s[:]), deps=dep_t, inc="s_pe")
                        ptk = b.emit("scalar", lambda e, kt=kt, p=p, gp=gp: e.copy(
                            out=pt_sb[gp][:, kt * 256 + p * 128:
                                           kt * 256 + p * 128 + 128],
                            in_=ptp[kt % 2][:, 0:128]), deps=[pe_t], inc="s_ac")
                        evs[kt % 2] = ptk
                    ptk_prev[gp][p] = ptk
                    pair_ptk.append(ptk)
                mmav = None
                for kt in range(j + 1):
                    mmav = b.emit("tensor", lambda e, kt=kt, g=g, j=j, gp=gp: e.matmul(
                        av_ps[gp][:], vbuf[kt][:, ts(g)],
                        pt_sb[gp][:, kt * 256: kt * 256 + 256],
                        start=(kt == 0), stop=(kt == j)),
                        deps=pair_ptk + [avc_prev[gp]],
                        inc="s_pe" if kt == j else None)
                av_prev[gp] = mmav
                avc = b.emit("scalar", lambda e, gp=gp: e.copy(
                    out=avst[gp][:], in_=av_ps[gp][:]),
                    deps=[mmav, store_prev], inc="s_ac")
                avc_prev[gp] = avc
                store_prev = b.dma("scalar",
                                   asp[4 * g: 4 * g + 4, :, ts(j, 64)].transpose([1, 0, 2]),
                                   avst[gp].ap().rearrange("p (c f) -> p c f", f=64),
                                   deps=[avc], stream="as0")
                attn_toks.append(store_prev)
        b.barrier(attn_toks)

    op_toks = []
    with ExitStack() as so:
        ao_t = mk(so, "ao_sb", [128, nch * 512], BF16)

        def ao_sb(ch):
            return ao_t[:, ch * 512:(ch + 1) * 512]
        wos = [mk(so, f"wos{i}", [128, 512], BF16) for i in range(2)]
        wof = [mk(so, f"wof{i}", [128, 512], F32) for i in range(2)]
        xo_sb = mk(so, "xo_sb", [128, 512], F32)
        op_ps = [mkp(so, f"op_ps{m}", [128, 512]) for m in range(4)]

        bank_last = {m: None for m in range(4)}
        wofprev = {}
        for mg in range(cfg.own // 512):
            ao_tok = b.dma("sync", ao_t.ap().rearrange("p (c f) -> p c f", f=512),
                           asp[:, :, ts(mg, 512)].transpose([1, 0, 2]),
                           deps=attn_toks[-1:] + [bank_last[m] for m in range(4)],
                           stream="ao")
            wlast = {0: None, 1: None}
            for og in range(nog):
                mmo = None
                for ho in range(nch):
                    wsb = wos[ho % 2]
                    if ho % 4 == 3:
                        wtok = b.dma("gpsimd", wsb[:], wo_t[ts(ho), ts(og, 512)],
                                     deps=[ao_tok, wlast[ho % 2]], stream="wog")
                    else:
                        lt = b.dma("sync", wof[ho % 2][:],
                                   wo_t[ts(ho), ts(og, 512)],
                                   deps=[ao_tok, wofprev.get(ho % 2)],
                                   stream=f"wo{ho % 2}")
                        wtok = b.emit("scalar", lambda e, ho=ho, wsb=wsb: e.copy(
                            out=wsb[:], in_=wof[ho % 2][:]),
                            deps=[lt, wlast[ho % 2]], inc="s_ac")
                        wofprev[ho % 2] = wtok
                    for m in range(4):
                        dep_m = [wtok, ao_tok]
                        if ho == 0:
                            dep_m.append(bank_last[m])
                        mmo = b.emit("tensor", lambda e, ho=ho, m=m, wsb=wsb: e.matmul(
                            op_ps[m][:], ao_sb(ho)[:, ts(m)], wsb[:],
                            start=(ho == 0), stop=(ho == nch - 1)),
                            deps=dep_m,
                            inc="s_pe" if m == 3 else None)
                    wlast[ho % 2] = mmo
                for m in range(4):
                    i_own = mg * 4 + m
                    g0, g1 = own_rows(xres, i_own, ts(og, 512))
                    b.dma("sync", xo_sb[0:64, :], g0, deps=[mmo] + op_toks[-1:],
                          stream="xo")
                    xl = b.dma("sync", xo_sb[64:128, :], g1,
                               deps=[mmo] + op_toks[-1:], stream="xo")
                    ad = b.emit("vector", lambda e, m=m: e.tensor_tensor(
                        out=xo_sb[:], in0=xo_sb[:], in1=op_ps[m][:], op=ALU.add),
                        deps=[xl, mmo], inc="s_dv")
                    bank_last[m] = ad
                    b.dma("scalar", g0, xo_sb[0:64, :], deps=[ad], stream="xs")
                    op_toks.append(b.dma("scalar", g1, xo_sb[64:128, :], deps=[ad],
                                         stream="xs"))
    b.barrier(op_toks)


def _softmax_pair(cfg, b, j, L, nchk, s_sb, p_sb, cm_sb, m8_sb, ta_sb,
                  mx_sb, rs_sb, mrec_s, mtok, ev_toks, p_war):
    cm = b.emit("vector", lambda e, L=L, nchk=nchk: e.tensor_reduce(
        out=cm_sb[:, 0:nchk],
        in_=s_sb.ap()[:, 0:L].rearrange("p (c k) -> p c k", k=16),
        op=ALU.max, axis=AX.X), deps=ev_toks, inc="s_dv")
    m8 = b.emit("vector", lambda e, nchk=nchk: e.max(
        out=m8_sb[:], in_=cm_sb[:, 0:nchk]), deps=[cm], inc="s_dv")
    ta = b.emit("vector", lambda e, nchk=nchk: e.tensor_scalar(
        out=ta_sb[:, 0:nchk], in0=cm_sb[:, 0:nchk], scalar1=m8_sb[:, 7:8],
        scalar2=NEGV, op0=ALU.is_lt, op1=ALU.mult), deps=[m8], inc="s_dv")
    deps_f = [ta]
    if j >= 2:
        if j >= 3:
            tav = ta_sb.ap()[:, 8:8 * (j - 1)].unsqueeze(2).broadcast_to(
                [128, 8 * (j - 1) - 8, 16])
            deps_f.append(b.emit("vector", lambda e, j=j, tav=tav: e.tensor_tensor(
                out=s_sb.ap()[:, 128:128 * (j - 1)].rearrange(
                    "p (c k) -> p c k", k=16),
                in0=s_sb.ap()[:, 128:128 * (j - 1)].rearrange(
                    "p (c k) -> p c k", k=16),
                in1=tav, op=ALU.add), deps=[ta], inc="s_dv"))
        tav2 = ta_sb.ap()[:, 8 * (j - 1):8 * j].unsqueeze(2).broadcast_to(
            [128, 8, 16])
        mx = b.emit("vector", lambda e, tav2=tav2: e.tensor_tensor(
            out=mx_sb.ap().rearrange("p (c k) -> p c k", k=16),
            in0=mrec_s.ap().rearrange("p (c k) -> p c k", k=16),
            in1=tav2, op=ALU.max), deps=[ta, mtok[1]], inc="s_dv")
        deps_f.append(b.emit("vector", lambda e, j=j: e.tensor_tensor(
            out=s_sb[:, 128 * (j - 1):128 * j],
            in0=s_sb[:, 128 * (j - 1):128 * j],
            in1=mx_sb[:], op=ALU.add), deps=[mx], inc="s_dv"))
    rm = b.emit("vector", lambda e, L=L: e.tensor_reduce(
        out=rs_sb[:, 0:1], in_=s_sb[:, 0:L], op=ALU.max, axis=AX.X,
        negate=True), deps=deps_f, inc="s_dv")
    ex = b.emit("scalar", lambda e, L=L: e.activation(
        out=p_sb[:, 0:L], in_=s_sb[:, 0:L], func=AF.Exp,
        bias=rs_sb[:, 0:1], accum_out=rs_sb[:, 1:2]),
        deps=[rm, p_war], inc="s_ac")
    rc = b.emit("vector", lambda e: e.reciprocal(
        out=rs_sb[:, 2:3], in_=rs_sb[:, 1:2]), deps=[ex], inc="s_dv")
    nm = b.emit("vector", lambda e, L=L: e.tensor_scalar(
        out=p_sb[:, 0:L], in0=p_sb[:, 0:L], scalar1=rs_sb[:, 2:3], scalar2=None,
        op0=ALU.mult), deps=[rc], inc="s_dv")
    return nm, ex


def _ffn_split(cfg, b, nc, stack, mk, mkp, ln_into, consts, *, biasb, wlt,
               lnTsp, src, dst, ntiles):
    d, nch, nog = cfg.d, cfg.nch, cfg.nog
    hog = nog // 2
    cols0 = hog * 512
    x_t = [mk(stack, f"fx_t{i}", [128, d], F32) for i in range(4)]
    ln_s = mk(stack, "fln_s", [128, d], BF16)
    lnT_t = mk(stack, "flnT", [128, nch * 512], BF16)
    w_sb = [mk(stack, f"fw_sb{i}", [128, 512], BF16) for i in range(2)]
    ev_s = mk(stack, "fev_s", [128, 512], F32)
    biasb_s = mk(stack, "fbias_s", [128, d], F32)
    ffn_ps = [mkp(stack, f"f_ps{m}", [128, 512]) for m in range(4)]
    ln_ps = [mkp(stack, f"fln_ps{i}", [128, 1024], BF16) for i in range(2)]

    bt = b.dma("sync", biasb_s[:], biasb, stream="c0")
    hb = b.emit("vector", lambda e: e.tensor_scalar(
        out=biasb_s[:], in0=biasb_s[:], scalar1=0.5, scalar2=None,
        op0=ALU.mult), deps=[bt], inc="s_dv")

    def lnT(ch):
        return lnT_t[:, ch * 512:(ch + 1) * 512]

    def lnT_view(ch, col):
        return lnT(ch)[:, ts(col)]

    st = dict(prev_half=[], lnr_guard=[], bank_guard={m: None for m in range(4)})

    def og_block(og, lnr, wlast, bank_last, deq_blk, res_toks):
        wdep = deq_blk[min(4 * og + 3, len(deq_blk) - 1)] if deq_blk else None
        mm_last = None
        for i in range(nch):
            wsb = w_sb[i % 2]
            wl_view = wlt[4 * og: 4 * og + 4, ts(i), :].transpose([1, 0, 2])
            wtok = b.dma("sync", wsb.ap().rearrange("p (bb f) -> p bb f", f=128),
                         wl_view, deps=[wdep, wlast[i % 2]] + lnr,
                         stream=f"fw{i % 2}")
            for m in range(4):
                dep_m = [wtok] + lnr
                if i == 0:
                    dep_m.append(bank_last[m] if bank_last[m] is not None
                                 else st["bank_guard"][m])
                mm_last = b.emit("tensor", lambda e, i=i, m=m, wsb=wsb: e.matmul(
                    ffn_ps[m][:], lnT(i)[:, ts(m)], wsb[:],
                    start=(i == 0), stop=(i == nch - 1)),
                    deps=dep_m, inc="s_pe" if m == 3 else None)
            wlast[i % 2] = mm_last
        for m in range(4):
            e1 = b.emit("vector", lambda e, m=m, og=og: e.scalar_tensor_tensor(
                out=ev_s[:], in0=ffn_ps[m][:], scalar=0.5,
                in1=biasb_s[:, ts(og, 512)], op0=ALU.mult, op1=ALU.add),
                deps=[mm_last, hb] + res_toks[-1:], inc="s_dv")
            e2 = b.emit("gpsimd", lambda e, m=m, og=og: e.tensor_tensor(
                out=x_t[m][:, ts(og, 512)], in0=x_t[m][:, ts(og, 512)],
                in1=ev_s[:], op=ALU.add), deps=[e1], inc="s_pl")
            bank_last[m] = e2
            res_toks.append(e2)
        return mm_last

    def pass0(deq_blk, after_og=None):
        slot = 0
        for hf in range(ntiles // 4):
            ln_toks = []
            res_toks = []
            for i in range(4):
                ld = b.dma("sync", x_t[i][:], src(hf * 4 + i),
                           deps=st["prev_half"] + ln_toks[-1:],
                           stream=f"fx{i % 2}")
                ln_toks.append(ln_into(x_t[i], ld, ln_s, ln_ps, lnT_view, i,
                                       deps=ln_toks[-1:] + st["lnr_guard"]))
            lnr = [ln_toks[-1]]
            wlast = {0: None, 1: None}
            bank_last = {m: None for m in range(4)}
            for og in range(hog):
                og_block(og, lnr, wlast, bank_last, deq_blk, res_toks)
                if after_og is not None:
                    after_og(slot)
                slot += 1
            sp = b.dma("scalar", lnTsp[hf], lnT_t[:], deps=lnr, stream="fsp")
            st["lnr_guard"] = [wlast[0], wlast[1], sp, res_toks[-1]]
            st["bank_guard"] = dict(bank_last)
            sts = []
            for i in range(4):
                sts.append(b.dma("scalar", dst(hf * 4 + i)[:, 0:cols0],
                                 x_t[i][:, 0:cols0], deps=res_toks,
                                 stream=f"fs{i % 2}"))
            st["prev_half"] = sts

    def pass1(deq_blk):
        all_st = []
        for hf in range(ntiles // 4):
            rl = b.dma("sync", lnT_t[:], lnTsp[hf], deps=st["lnr_guard"],
                       stream="frl")
            lnr = [rl]
            ld_last = [None, None]
            for i in range(4):
                ld_last[i % 2] = b.dma("sync", x_t[i][:, cols0:d],
                                       src(hf * 4 + i)[:, cols0:d],
                                       deps=st["prev_half"], stream=f"fx{i % 2}")
            lnr += ld_last
            res_toks = []
            wlast = {0: None, 1: None}
            bank_last = {m: None for m in range(4)}
            for og in range(hog, nog):
                og_block(og, lnr, wlast, bank_last, deq_blk, res_toks)
            st["lnr_guard"] = [wlast[0], wlast[1], res_toks[-1]]
            st["bank_guard"] = dict(bank_last)
            sts = []
            for i in range(4):
                sts.append(b.dma("scalar", dst(hf * 4 + i)[:, cols0:d],
                                 x_t[i][:, cols0:d], deps=res_toks,
                                 stream=f"fs{i % 2}"))
            st["prev_half"] = sts
            all_st += sts
        return all_st

    return pass0, pass1


def _ffn_sublayer(cfg, b, nc, stack, mk, mkp, ln_into, consts, biasb, wlt,
                  deq_blk, src_tile, dst_tile, ntiles, pre="f"):
    d, nch, nog = cfg.d, cfg.nch, cfg.nog
    x_t = [mk(stack, pre + f"x_t{i}", [128, d], F32) for i in range(4)]
    ln_s = mk(stack, pre + "ln_s", [128, d], BF16)
    lnT_t = mk(stack, pre + "lnT", [128, nch * 512], BF16)

    def lnT(ch):
        return lnT_t[:, ch * 512:(ch + 1) * 512]
    w_sb = [mk(stack, pre + f"w_sb{i}", [128, 512], BF16) for i in range(2)]
    ev_s = mk(stack, pre + "ev_s", [128, 512], F32)
    biasb_s = mk(stack, pre + "bias_s", [128, d], F32)
    ffn_ps = [mkp(stack, pre + f"_ps{m}", [128, 512]) for m in range(4)]
    ln_ps = [mkp(stack, pre + f"ln_ps{i}", [128, 1024], BF16) for i in range(2)]

    bt = b.dma("sync", biasb_s[:], biasb, stream="c0")
    hb = b.emit("vector", lambda e: e.tensor_scalar(
        out=biasb_s[:], in0=biasb_s[:], scalar1=0.5, scalar2=None,
        op0=ALU.mult), deps=[bt], inc="s_dv")

    def lnT_view(ch, col):
        return lnT(ch)[:, ts(col)]

    all_st = []
    prev_half = []
    for hf in range(ntiles // 4):
        ln_toks = []
        ld_toks = []
        for i in range(4):
            stile = src_tile(hf * 4 + i)
            if isinstance(stile, list):
                b.dma("sync", x_t[i][0:64, :], stile[0],
                      deps=prev_half + ln_toks[-1:], stream=pre + f"x{i}")
                ld = b.dma("sync", x_t[i][64:128, :], stile[1],
                           deps=prev_half + ln_toks[-1:], stream=pre + f"x{i}")
            else:
                ld = b.dma("sync", x_t[i][:], stile,
                           deps=prev_half + ln_toks[-1:], stream=pre + f"x{i}")
            ld_toks.append(ld)
            ln_toks.append(ln_into(x_t[i], ld, ln_s, ln_ps, lnT_view, i,
                                   deps=ln_toks[-1:]))
        lnr = [ln_toks[-1]]
        res_toks = []
        bank_last = {m: None for m in range(4)}
        wlast = {0: None, 1: None}
        for og in range(nog):
            wdep = deq_blk[min(4 * og + 3, len(deq_blk) - 1)] if deq_blk else None
            mm_last = None
            for i in range(nch):
                wsb = w_sb[i % 2]
                wl_view = wlt[4 * og: 4 * og + 4, ts(i), :].transpose([1, 0, 2])
                wtok = b.dma("sync", wsb.ap().rearrange("p (b f) -> p b f", f=128),
                             wl_view,
                             deps=[wdep, wlast[i % 2]] + lnr, stream=pre + f"w{i % 2}")
                for m in range(4):
                    dep_m = [wtok] + lnr
                    if i == 0 and bank_last[m] is not None:
                        dep_m.append(bank_last[m])
                    mm_last = b.emit("tensor", lambda e, i=i, m=m, wsb=wsb: e.matmul(
                        ffn_ps[m][:], lnT(i)[:, ts(m)], wsb[:],
                        start=(i == 0), stop=(i == nch - 1)),
                        deps=dep_m,
                        inc="s_pe" if m == 3 else None)
                wlast[i % 2] = mm_last
            for m in range(4):
                e1 = b.emit("vector", lambda e, m=m, og=og: e.scalar_tensor_tensor(
                    out=ev_s[:], in0=ffn_ps[m][:], scalar=0.5,
                    in1=biasb_s[:, ts(og, 512)], op0=ALU.mult, op1=ALU.add),
                    deps=[mm_last, hb] + res_toks[-1:], inc="s_dv")
                e2 = b.emit("gpsimd", lambda e, m=m, og=og: e.tensor_tensor(
                    out=x_t[m][:, ts(og, 512)], in0=x_t[m][:, ts(og, 512)],
                    in1=ev_s[:], op=ALU.add), deps=[e1], inc="s_pl")
                bank_last[m] = e2
                res_toks.append(e2)
        st_toks = []
        for i in range(4):
            dtile = dst_tile(hf * 4 + i)
            if isinstance(dtile, list):
                b.dma("scalar", dtile[0], x_t[i][0:64, :], deps=res_toks,
                      stream=pre + f"s{i % 2}")
                st_toks.append(b.dma("scalar", dtile[1], x_t[i][64:128, :],
                                     deps=res_toks, stream=pre + f"s{i % 2}"))
            else:
                st_toks.append(b.dma("scalar", dtile, x_t[i][:], deps=res_toks,
                                     stream=pre + f"s{i % 2}"))
        all_st += st_toks
        prev_half = st_toks
    return all_st


def shard_inputs(cfg, x_full, qweight, lut4, bias, wq, wk, wv, wo):
    wq_t = np.ascontiguousarray(wq.T)
    wk_t = np.ascontiguousarray(wk.T)
    wv_t = np.ascontiguousarray(wv.T)
    wo_t = np.ascontiguousarray(wo.T)
    qw4 = np.ascontiguousarray(qweight[0:4])
    biasb = np.broadcast_to(bias[None, :], (128, cfg.d)).copy()
    in_maps = []
    for c in range(8):
        bb, half = c // 2, c % 2
        t = make_tables(cfg, half)
        in_maps.append(dict(
            xin=permute_seq(x_full[bb], half),
            qw4=qw4, lut4=np.ascontiguousarray(lut4), biasb=biasb,
            wq_t=wq_t, wk_t=wk_t, wv_t=wv_t, wo_t=wo_t,
            mask32=t["mask32"], ident=t["ident"],
            cos_k=t["cos_k"], sin_k=t["sin_k"],
            cos_q=t["cos_q"], sin_q=t["sin_q"],
            m_end=t["m_end"], m_rec=t["m_rec"]))
    return in_maps


def assemble_output(cfg, results):
    out = np.zeros((4, cfg.seq, cfg.d), dtype=np.float32)
    o = np.arange(cfg.own)
    for c in range(8):
        bb, half = c // 2, c % 2
        rows = np.asarray(results[c]["xout"])
        out[bb, 128 * (o // 64) + 64 * half + (o % 64), :] = rows
    return out


def kernel(L0_attn_hidden_states, L0_attn_wq, L0_attn_wk, L0_attn_wv, L0_attn_wo,
           L0_ffn_qweight, L0_ffn_lut4, L0_ffn_lut8, L0_ffn_bias):
    cfg = Cfg()
    nc = build_nc(cfg)
    in_maps = shard_inputs(cfg, np.asarray(L0_attn_hidden_states),
                           np.asarray(L0_ffn_qweight), np.asarray(L0_ffn_lut4),
                           np.asarray(L0_ffn_bias), np.asarray(L0_attn_wq),
                           np.asarray(L0_attn_wk), np.asarray(L0_attn_wv),
                           np.asarray(L0_attn_wo))
    res = run_bass_kernel_spmd(nc, in_maps, core_ids=list(range(8)))
    return assemble_output(cfg, res.results)


# version 11
# speedup vs baseline: 1.1658x; 1.1658x over previous
import math
import numpy as np
import ml_dtypes
import concourse.bass as bass
import concourse.mybir as mybir
from concourse.bass_utils import run_bass_kernel_spmd

F32 = mybir.dt.float32
BF16 = mybir.dt.bfloat16
I32 = mybir.dt.int32
AF = mybir.ActivationFunctionType
ALU = mybir.AluOpType
AX = mybir.AxisListType

NEGV = -1.0e30
EPS = 1e-5


def ts(i, n=128):
    return slice(i * n, (i + 1) * n)


class Cfg:
    def __init__(self, d=4096, seq=2048, sub_b=True, sub_c=True,
                 b_stop="full", dbg=False, attn_iters=None):
        self.b_stop = b_stop
        self.dbg = dbg
        self.attn_iters = attn_iters
        self.d = d
        self.seq = seq
        self.kvd = d // 4
        self.heads = d // 128
        self.kvheads = self.kvd // 128
        self.words = d // 32
        self.nblk = d // 128
        self.nch = d // 128
        self.qt = seq // 128
        self.own = seq // 2
        self.nog = d // 512
        self.sub_b = sub_b
        self.sub_c = sub_c
        assert d % 512 == 0 and seq % 1024 == 0


class B:
    ENGS = ("sync", "scalar", "vector", "tensor", "gpsimd")

    def __init__(self, nc):
        self.nc = nc
        self.streams = {k: [] for k in self.ENGS}
        self.sems = {}
        self.counts = {}
        self.waited = {k: {} for k in self.ENGS}
        self.pending = {k: [] for k in self.ENGS}
        self.lastdma = {}
        self.semfactory = None

    def sem(self, name):
        self.counts[name] = 0

    def barrier(self, toks):
        toks = [t for t in toks if t is not None]
        for k in self.ENGS:
            self.pending[k] = list(toks)

    def emit(self, eng, fn, deps=(), inc=None, dinc=None):
        deps = list(deps) + self.pending[eng]
        self.pending[eng] = []
        waits = []
        for t in deps:
            if t is None:
                continue
            s, v = t
            if v <= 0:
                continue
            w = self.waited[eng]
            if w.get(s, -1) >= v:
                continue
            w[s] = v
            waits.append((s, v))
        out_tok = None
        inc_s, amt = None, 0
        if inc is not None:
            self.counts[inc] += 1
            out_tok = (inc, self.counts[inc])
            inc_s, amt = inc, 1
        elif dinc is not None:
            self.counts[dinc] += 16
            out_tok = (dinc, self.counts[dinc])
            inc_s, amt = dinc, 16
        self.streams[eng].append((fn, waits, inc_s, amt))
        return out_tok

    def dma(self, queue, out, in_, deps=(), stream="misc", **kw):
        sem = "d_" + stream
        if sem not in self.counts:
            self.counts[sem] = 0
            self.sems[sem] = self.semfactory(sem)
            self.lastdma[sem] = None
        deps = list(deps) + [self.lastdma[sem]]
        tok = self.emit(queue, lambda e: e.dma_start(out=out, in_=in_, **kw),
                        deps=deps, dinc=sem)
        self.lastdma[sem] = tok
        return tok

    def build(self, block):
        semobjs = self.sems

        def run(eng_name, engine):
            for (fn, waits, inc_s, amt) in self.streams[eng_name]:
                for (s, v) in waits:
                    engine.wait_ge(semobjs[s], v)
                ins = fn(engine)
                if inc_s is not None:
                    ins.then_inc(semobjs[inc_s], amt)

        @block.sync
        def _(e):
            run("sync", e)

        @block.scalar
        def _(e):
            run("scalar", e)

        @block.vector
        def _(e):
            run("vector", e)

        @block.tensor
        def _(e):
            run("tensor", e)

        @block.gpsimd
        def _(e):
            run("gpsimd", e)


def make_tables(cfg, half):
    t = {}
    t["mask32"] = np.broadcast_to((np.int32(1) << np.arange(32, dtype=np.int32)),
                                  (128, 32)).copy()
    t["ident"] = np.eye(128, dtype=ml_dtypes.bfloat16)
    invf = 1.0 / (10000.0 ** (np.arange(0, 128, 2, dtype=np.float64) / 128.0))
    invf_ext = np.concatenate([invf, invf])
    u = np.arange(128)
    pb = (u + 64 * half) % 128
    blocks = np.arange(cfg.seq // 128)
    seq_of_stored = (128 * blocks[:, None] + pb[None, :]).reshape(-1)
    ang = invf_ext[:, None] * seq_of_stored[None, :].astype(np.float64)
    t["cos_k"] = np.cos(ang).astype(np.float32)
    t["sin_k"] = np.sin(ang).astype(np.float32)
    o = np.arange(cfg.own)
    own_seq = 128 * (o // 64) + 64 * half + (o % 64)
    ang_q = invf_ext[:, None] * own_seq[None, :].astype(np.float64)
    sc = 1.0 / math.sqrt(128.0)
    t["cos_q"] = (np.cos(ang_q) * sc).astype(np.float32)
    t["sin_q"] = (np.sin(ang_q) * sc).astype(np.float32)
    r = np.arange(64)
    qpos_in_blk = 64 * half + r
    m = np.where(pb[None, :] <= qpos_in_blk[:, None], 0.0, NEGV).astype(np.float32)
    t["m_end"] = np.concatenate([m, m], axis=0)
    mr = np.where(pb[None, :] > qpos_in_blk[:, None], 0.0, NEGV).astype(np.float32)
    t["m_rec"] = np.concatenate([mr, mr], axis=0)
    return t


def permute_seq(x, half):
    if half == 0:
        return np.ascontiguousarray(x)
    S, d = x.shape
    v = x.reshape(S // 128, 2, 64, d)
    return np.ascontiguousarray(v[:, ::-1].reshape(S, d))


def build_nc(cfg):
    nc = bass.Bass("TRN2", target_bir_lowering=False)
    d, S = cfg.d, cfg.seq
    nch, nog, kvh = cfg.nch, cfg.nog, cfg.kvheads

    xin = nc.dram_tensor("xin", [S, d], F32, kind="ExternalInput").ap()
    qw4 = nc.dram_tensor("qw4", [4, d, cfg.words], I32, kind="ExternalInput").ap()
    lut4 = nc.dram_tensor("lut4", [d, 16], F32, kind="ExternalInput").ap()
    biasb = nc.dram_tensor("biasb", [128, d], F32, kind="ExternalInput").ap()
    wq_t = nc.dram_tensor("wq_t", [d, d], F32, kind="ExternalInput").ap()
    wk_t = nc.dram_tensor("wk_t", [d, cfg.kvd], F32, kind="ExternalInput").ap()
    wv_t = nc.dram_tensor("wv_t", [d, cfg.kvd], F32, kind="ExternalInput").ap()
    wo_t = nc.dram_tensor("wo_t", [d, d], F32, kind="ExternalInput").ap()
    mask32 = nc.dram_tensor("mask32", [128, 32], I32, kind="ExternalInput").ap()
    ident_in = nc.dram_tensor("ident", [128, 128], BF16, kind="ExternalInput").ap()
    cos_k = nc.dram_tensor("cos_k", [128, S], F32, kind="ExternalInput").ap()
    sin_k = nc.dram_tensor("sin_k", [128, S], F32, kind="ExternalInput").ap()
    cos_q = nc.dram_tensor("cos_q", [128, cfg.own], F32, kind="ExternalInput").ap()
    sin_q = nc.dram_tensor("sin_q", [128, cfg.own], F32, kind="ExternalInput").ap()
    m_end = nc.dram_tensor("m_end", [128, 128], F32, kind="ExternalInput").ap()
    m_rec = nc.dram_tensor("m_rec", [128, 128], F32, kind="ExternalInput").ap()

    xout = nc.dram_tensor("xout", [cfg.own, d], F32, kind="ExternalOutput").ap()

    wlt = nc.dram_tensor("wlt", [cfg.nblk, d, 128], BF16).ap()
    lnTsp = nc.dram_tensor("lnTsp", [S // 512, 128, nch * 512], BF16).ap()
    xres = nc.dram_tensor("xres", [S, d], F32).ap()
    qsp = nc.dram_tensor("qsp", [nch, 128, cfg.own], BF16).ap()
    asp = nc.dram_tensor("asp", [nch, 128, cfg.own], BF16).ap()

    def own_rows(src, i, cols=None):
        v = src.rearrange("(a r) d -> a r d", r=64)
        if cols is not None:
            v = v[:, :, cols]
        return [v[4 * i + 2 * a] for a in range(2)]

    def dma_own(queue, sbuf_tile, src, i, cols=None, write=False, deps=(),
                stream="misc"):
        g0, g1 = own_rows(src, i, cols)
        if write:
            b.dma(queue, g0, sbuf_tile[0:64, :], deps=deps, stream=stream)
            return b.dma(queue, g1, sbuf_tile[64:128, :], deps=deps, stream=stream)
        b.dma(queue, sbuf_tile[0:64, :], g0, deps=deps, stream=stream)
        return b.dma(queue, sbuf_tile[64:128, :], g1, deps=deps, stream=stream)

    b = B(nc)
    for s in ("s_pe", "s_dv", "s_ac", "s_pl"):
        b.sem(s)

    from contextlib import ExitStack
    with ExitStack() as root:
        def mk(stack, name, shape, dt):
            return stack.enter_context(nc.sbuf_tensor(name, shape, dt))

        def mkp(stack, name, shape, dt=F32):
            return stack.enter_context(nc.psum_tensor(name, shape, dt))

        for s in list(b.counts):
            b.sems[s] = root.enter_context(nc.semaphore(s))
        b.semfactory = lambda name: root.enter_context(nc.semaphore(name))
        block = root.enter_context(nc.Block())

        mask32_s = mk(root, "mask32_s", [128, 32], I32)
        ident_s = mk(root, "ident_s", [128, 128], BF16)
        stat_s = mk(root, "stat_s", [128, 64], F32)
        mv_s = mk(root, "mv_s", [128, 8], F32)
        eps_s = mk(root, "eps_s", [128, 1], F32)

        c_toks = [b.dma("sync", mask32_s[:], mask32, stream="c0"),
                  b.dma("sync", ident_s[:], ident_in, stream="c0"),
                  b.emit("vector", lambda e: e.memset(eps_s[:], EPS), inc="s_dv")]
        consts = tuple(c_toks)

        def ln_into(xt_tile, ld_tok, ln_s, ln_ps, lnT_view, col, deps=()):
            nst = d // 512
            st_toks = []
            for k in range(nst):
                st_toks.append(b.emit("vector", lambda e, k=k: e.bn_stats(
                    out=stat_s[:, 6 * k:6 * (k + 1)], in_=xt_tile[:, ts(k, 512)]),
                    deps=[ld_tok] + list(deps), inc="s_dv"))
            ag = b.emit("vector", lambda e, n=nst: e.bn_aggr(
                out=mv_s[:, 0:2], in_=stat_s[:, 0:6 * n]), deps=st_toks, inc="s_dv")
            l1 = b.emit("scalar", lambda e: e.activation(
                out=mv_s[:, 2:3], in_=mv_s[:, 1:2], func=AF.Ln, bias=eps_s[:]),
                deps=[ag] + list(consts), inc="s_ac")
            l2 = b.emit("scalar", lambda e: e.activation(
                out=mv_s[:, 3:4], in_=mv_s[:, 2:3], func=AF.Exp, scale=-0.5),
                deps=[l1], inc="s_ac")
            ap_tok = b.emit("vector", lambda e: e.tensor_scalar(
                out=ln_s[:], in0=xt_tile[:], scalar1=mv_s[:, 0:1],
                scalar2=mv_s[:, 3:4], op0=ALU.subtract, op1=ALU.mult),
                deps=[l2], inc="s_dv")
            evh = []
            for ch in range(nch):
                dep_t = [ap_tok] + list(consts)
                if len(evh) >= 2:
                    dep_t.append(evh[-2])
                pe_tok = b.emit("tensor", lambda e, ch=ch: e.transpose(
                    ln_ps[ch % 2][:, 0:128], ln_s[:, ts(ch)], ident_s[:]),
                    deps=dep_t, inc="s_pe")
                evh.append(b.emit("scalar", lambda e, ch=ch: e.copy(
                    out=lnT_view(ch, col), in_=ln_ps[ch % 2][:, 0:128]),
                    deps=[pe_tok], inc="s_ac"))
            return evh[-1]

        deq_blk = []
        with ExitStack() as sa:
            qw_s2 = [mk(sa, f"qw_s{i}", [128, 4 * cfg.words], I32) for i in range(2)]
            lut_s2 = [mk(sa, f"lut_s{i}", [128, 16], F32) for i in range(2)]
            idx_s2 = [mk(sa, f"idx_s{i}", [128, d], BF16) for i in range(2)]
            sgn_s2 = [mk(sa, f"sgn_s{i}", [128, d], BF16) for i in range(2)]
            and2 = [mk(sa, f"and_s{i}", [128, d // 2], I32) for i in range(2)]
            acc_s = mk(sa, "acc_s", [128, d], BF16)
            wtst_s = mk(sa, "wtst_s", [128, d], BF16)
            tp_ps2 = [mkp(sa, f"tp_ps{i}", [128, 1024], BF16) for i in range(2)]

            mask_view2 = mask32_s.ap().unsqueeze(1).broadcast_to(
                [128, cfg.words // 2, 32])

            dstate = dict(last_tp=None, sgn_by_buf={0: None, 1: None})
            prev_sel = [None, None]
            prev_idx = [[], []]

            dloads = {}

            def emit_load(blk):
                if blk in dloads or blk >= cfg.nblk:
                    return
                par = blk % 2
                qtok = b.dma("scalar",
                             qw_s2[par].ap().rearrange("p (a w) -> p a w", a=4),
                             qw4[:, ts(blk), :].transpose([1, 0, 2]),
                             deps=list(consts) + prev_idx[par],
                             stream=f"qw{par}")
                ltok = b.dma("scalar", lut_s2[par][:], lut4[ts(blk), :],
                             deps=list(consts) + [prev_sel[par]],
                             stream=f"lut{par}")
                dloads[blk] = (qtok, ltok)

            def emit_deq(blk):
                par = blk % 2
                qws, luts, idxs, sgns = (qw_s2[par], lut_s2[par],
                                         idx_s2[par], sgn_s2[par])
                emit_load(blk)
                qtok, ltok = dloads.pop(blk)
                emit_load(blk + 1)
                hwd = d // 2
                hwords = cfg.words // 2
                idx_half = [None, None]
                sign_of = {}

                def emit_and(s):
                    p, h = s // 2, s % 2
                    s_i = s % 2
                    ab = and2[s_i]
                    w0 = p * cfg.words + h * hwords
                    qv = qws.ap()[:, w0:w0 + hwords].unsqueeze(2).broadcast_to(
                        [128, hwords, 32])
                    a_tok = b.emit("vector", lambda e, qv=qv, ab=ab: e.tensor_tensor(
                        out=ab.ap().rearrange("p (w t) -> p w t", t=32),
                        in0=qv, in1=mask_view2, op=ALU.bitwise_and),
                        deps=[qtok, dstate["sgn_by_buf"][s_i]] + list(consts),
                        inc="s_dv")
                    dst_t = idxs if p == 0 else sgns
                    s_tok = b.emit("scalar", lambda e, ab=ab, dst_t=dst_t, h=h: e.activation(
                        out=dst_t[:, h * hwd:(h + 1) * hwd], in_=ab[:],
                        func=AF.Sign),
                        deps=[a_tok, prev_sel[par]], inc="s_ac")
                    dstate["sgn_by_buf"][s_i] = s_tok
                    sign_of[s] = s_tok
                    if p == 0:
                        idx_half[h] = s_tok

                def emit_stt(s):
                    p, h = s // 2, s % 2
                    idx_half[h] = b.emit("vector", lambda e, p=p, h=h, idxs=idxs, sgns=sgns: e.scalar_tensor_tensor(
                        out=idxs[:, h * hwd:(h + 1) * hwd],
                        in0=sgns[:, h * hwd:(h + 1) * hwd],
                        scalar=float(1 << p),
                        in1=idxs[:, h * hwd:(h + 1) * hwd],
                        op0=ALU.mult, op1=ALU.add),
                        deps=[sign_of[s], idx_half[h]], inc="s_dv")

                for step in (('A', 0), ('A', 1), ('A', 2), ('A', 3), ('T', 2),
                             ('A', 4), ('T', 3), ('A', 5), ('T', 4), ('A', 6),
                             ('T', 5), ('A', 7), ('T', 6), ('T', 7)):
                    (emit_and if step[0] == 'A' else emit_stt)(step[1])
                prev_idx[par] = list(idx_half)
                idx_tok = idx_half
                acc_tok = b.emit("vector", lambda e, idxs=idxs, luts=luts: e.tensor_scalar(
                    out=acc_s[:], in0=idxs[:], scalar1=0.0,
                    scalar2=luts[:, 0:1], op0=ALU.is_equal, op1=ALU.mult),
                    deps=idx_tok + [ltok, dstate["last_tp"]], inc="s_dv")
                for c in range(1, 16):
                    t_tok = b.emit("vector", lambda e, c=c, idxs=idxs, luts=luts, sgns=sgns: e.tensor_scalar(
                        out=sgns[:], in0=idxs[:],
                        scalar1=float(c), scalar2=luts[:, c:c + 1],
                        op0=ALU.is_equal, op1=ALU.mult),
                        deps=idx_tok + [ltok, acc_tok], inc="s_dv")
                    acc_tok = b.emit("vector", lambda e, sgns=sgns: e.tensor_tensor(
                        out=acc_s[:], in0=acc_s[:], in1=sgns[:], op=ALU.add),
                        deps=[t_tok], inc="s_dv")
                prev_sel[par] = acc_tok
                evt_hist = []
                for ch in range(nch):
                    dep_t = [acc_tok] + list(consts)
                    if ch == 0 and deq_blk:
                        dep_t.append(deq_blk[-1])
                    if len(evt_hist) >= 2:
                        dep_t.append(evt_hist[-2])
                    pe_tok = b.emit("tensor", lambda e, ch=ch: e.transpose(
                        tp_ps2[ch % 2][:, 0:128], acc_s[:, ts(ch)], ident_s[:]),
                        deps=dep_t, inc="s_pe")
                    evt_hist.append(b.emit("scalar", lambda e, ch=ch: e.copy(
                        out=wtst_s[:, ts(ch)], in_=tp_ps2[ch % 2][:, 0:128]),
                        deps=[pe_tok], inc="s_ac"))
                dstate["last_tp"] = pe_tok
                st = b.dma("scalar",
                           wlt[blk].rearrange("(c p) f -> c p f", p=128).transpose([1, 0, 2]),
                           wtst_s.ap().rearrange("p (c f) -> p c f", f=128),
                           deps=[evt_hist[-1]], stream="dst")
                deq_blk.append(st)

            ffn_p0, ffn_p1 = _ffn_split(
                cfg, b, nc, sa, mk, mkp, ln_into, consts,
                biasb=biasb, wlt=wlt, lnTsp=lnTsp,
                src=lambda i: xin[ts(i), :],
                dst=lambda i: xres[ts(i), :], ntiles=S // 128)

            H = cfg.nblk // 2
            emit_load(0)
            for blk in range(H):
                emit_deq(blk)
            rest = list(range(H, cfg.nblk))
            n_slots = (S // 512) * (cfg.nog // 2)
            di = [0]

            def after_og(slot):
                want = min((slot + 1) * len(rest) // n_slots, len(rest))
                while di[0] < want:
                    emit_deq(rest[di[0]])
                    di[0] += 1
            ffn_p0(deq_blk, after_og)
            while di[0] < len(rest):
                emit_deq(rest[di[0]])
                di[0] += 1
            ffn_toks = ffn_p1(deq_blk)
        b.barrier(ffn_toks + [deq_blk[-1]])

        final_deps = ffn_toks

        if cfg.sub_b:
            _sublayer_b(cfg, b, nc, mk, mkp, ln_into, consts,
                        xres=xres, qsp=qsp, asp=asp, own_rows=own_rows,
                        wq_t=wq_t, wk_t=wk_t, wv_t=wv_t, wo_t=wo_t,
                        cos_q=cos_q, sin_q=sin_q, cos_k=cos_k, sin_k=sin_k,
                        m_end=m_end, m_rec=m_rec, ident_s=ident_s,
                        final_deps=final_deps)
            final_deps = []

        if cfg.sub_c:
            with ExitStack() as sc:
                ffn2 = _ffn_sublayer(
                    cfg, b, nc, sc, mk, mkp, ln_into, consts,
                    biasb=biasb, wlt=wlt, deq_blk=deq_blk,
                    src_tile=lambda i: own_rows(xres, i),
                    dst_tile=lambda i: xout[ts(i), :],
                    ntiles=cfg.own // 128, pre="c")
            final_deps = ffn2
        else:
            cp = []
            if True:
                cp_t = mk(root, "cp_t", [128, d], F32)
                for i in range(cfg.own // 128):
                    t1 = dma_own("sync", cp_t, xres, i,
                                 deps=final_deps + cp[-1:], stream="cp")
                    cp.append(b.dma("scalar", xout[ts(i), :], cp_t[:], deps=[t1],
                                    stream="cps"))
            final_deps = cp

        b.emit("scalar", lambda e: e.nop(), deps=final_deps)

        b.build(block)
    return nc


def _sublayer_b(cfg, b, nc, mk, mkp, ln_into, consts, *, xres, qsp, asp,
                own_rows, wq_t, wk_t, wv_t, wo_t, cos_q, sin_q, cos_k, sin_k,
                m_end, m_rec, ident_s, final_deps):
    from contextlib import ExitStack
    d, S, nch, nog, kvh = cfg.d, cfg.seq, cfg.nch, cfg.nog, cfg.kvheads
    VW = min(512, cfg.kvd)

    with ExitStack() as skv:
        kbuf = [mk(skv, f"kbuf{g}", [128, S], BF16) for g in range(kvh)]
        vbuf = [mk(skv, f"vbuf{t}", [128, cfg.kvd], BF16) for t in range(S // 128)]
        proj_toks = []
        with ExitStack() as sp:
            xb_t = [mk(sp, f"xb_t{i}", [128, d], F32) for i in range(2)]
            ln_s = mk(sp, "ln2_s", [128, d], BF16)
            ln2T_t = mk(sp, "ln2T", [128, nch * 512], BF16)

            def ln2T(ch):
                return ln2T_t[:, ch * 512:(ch + 1) * 512]
            cosq_s = mk(sp, "cosq_s", [128, cfg.own], F32)
            sinq_s = mk(sp, "sinq_s", [128, cfg.own], F32)
            cosk_s = mk(sp, "cosk_s", [128, S], F32)
            sink_s = mk(sp, "sink_s", [128, S], F32)
            wcol_s = [mk(sp, f"wcol_s{i}", [128, nch * 128], BF16) for i in range(2)]
            wf32 = [mk(sp, f"wf32_{i}", [128, d // 2], F32) for i in range(2)]
            rsc = mk(sp, "rsc", [128, 1024], F32)
            qstg2 = [mk(sp, f"qstg{i}", [128, 256], BF16) for i in range(2)]
            ln_ps = [mkp(sp, f"ln2_ps{i}", [128, 1024], BF16) for i in range(2)]
            pp = [mkp(sp, "pr_ps", [128, 512]), mkp(sp, "pr_ps2", [128, 512])]

            rt = [b.dma("sync", cosq_s[:], cos_q, stream="c0"),
                  b.dma("sync", sinq_s[:], sin_q, stream="c0"),
                  b.dma("sync", cosk_s[:], cos_k, stream="c0"),
                  b.dma("sync", sink_s[:], sin_k, stream="c0")]
            rope_tok = rt[-1]

            def lnT_view(ch, col):
                return ln2T(ch)[:, ts(col)]

            ps_free = [None, None]
            wprev = {}
            wfprev = {}
            qst_prev = {}
            last_rope = [None]
            hc = nch // 2

            def load_wcol(w_ap, oc, par, deps):
                wsb = wcol_s[par]
                if oc % 4 == 3:
                    wtok = b.dma("gpsimd",
                                 wsb.ap().rearrange("p (c f) -> p c f", f=128),
                                 w_ap[:, ts(oc)].rearrange(
                                     "(c p) f -> c p f", p=128).transpose([1, 0, 2]),
                                 deps=deps + [wprev.get(par)], stream="bwg")
                    return [wtok, wtok]
                cts = []
                for hv in range(2):
                    lv = w_ap[hv * (d // 2):(hv + 1) * (d // 2), ts(oc)].rearrange(
                        "(c p) f -> c p f", p=128).transpose([1, 0, 2])
                    lt = b.dma("sync",
                               wf32[par].ap().rearrange("p (c f) -> p c f", f=128),
                               lv, deps=deps + [wfprev.get(par)],
                               stream=f"bw{par}")
                    ct = b.emit("scalar", lambda e, par=par, hv=hv, wsb=wsb: e.copy(
                        out=wsb[:, hv * hc * 128:(hv + 1) * hc * 128],
                        in_=wf32[par][:]),
                        deps=[lt, wprev.get(par)], inc="s_ac")
                    wfprev[par] = ct
                    cts.append(ct)
                return cts
            for hf in range(S // 512):
                lt_done = []
                for i in range(4):
                    tile = hf * 4 + i
                    ld = b.dma("sync", xb_t[i % 2][:], xres[ts(tile), :],
                               deps=final_deps + lt_done[-2:] + proj_toks[-1:],
                               stream=f"bx{i % 2}")
                    lt_done.append(ln_into(xb_t[i % 2], ld, ln_s, ln_ps,
                                           lnT_view, i, deps=lt_done[-1:]))
                lnr = [lt_done[-1]]

                def own_view(ch):
                    v = ln2T(ch).rearrange("p (a g r) -> p a g r", g=2, r=64)
                    return v[:, :, 0]

                own0 = hf * 256
                for oc in range(nch):
                    qstg = qstg2[oc % 2]
                    wsb = wcol_s[oc % 2]
                    cts = load_wcol(wq_t, oc, oc % 2, lnr)
                    pps = pp[oc % 2]
                    mm = None
                    for i in range(nch):
                        mm = b.emit("tensor", lambda e, i=i, wsb=wsb, pps=pps: e.matmul(
                            pps[:, 0:256], wsb[:, ts(i)], own_view(i),
                            start=(i == 0), stop=(i == nch - 1)),
                            deps=[cts[i // hc], ps_free[oc % 2]] + lnr,
                            inc="s_pe" if i == nch - 1 else None)
                    wprev[oc % 2] = mm
                    csl = slice(own0, own0 + 256)
                    r1 = b.emit("vector", lambda e, pps=pps, csl=csl: e.tensor_tensor(
                        out=rsc[0:64, 0:256], in0=pps[0:64, 0:256],
                        in1=cosq_s[0:64, csl], op=ALU.mult),
                        deps=[mm, rope_tok, last_rope[0]], inc="s_dv")
                    r2 = b.emit("vector", lambda e, pps=pps, csl=csl: e.tensor_tensor(
                        out=rsc[0:64, 256:512], in0=pps[64:128, 0:256],
                        in1=sinq_s[0:64, csl], op=ALU.mult),
                        deps=[mm, last_rope[0]], inc="s_dv")
                    r3 = b.emit("vector", lambda e, qstg=qstg: e.tensor_tensor(
                        out=qstg[0:64, 0:256], in0=rsc[0:64, 0:256],
                        in1=rsc[0:64, 256:512], op=ALU.subtract),
                        deps=[r1, r2, qst_prev.get(oc % 2)], inc="s_dv")
                    r4 = b.emit("vector", lambda e, pps=pps, csl=csl: e.tensor_tensor(
                        out=rsc[0:64, 512:768], in0=pps[64:128, 0:256],
                        in1=cosq_s[64:128, csl], op=ALU.mult),
                        deps=[mm, last_rope[0]], inc="s_dv")
                    r5 = b.emit("vector", lambda e, pps=pps, csl=csl: e.tensor_tensor(
                        out=rsc[0:64, 768:1024], in0=pps[0:64, 0:256],
                        in1=sinq_s[64:128, csl], op=ALU.mult),
                        deps=[mm, last_rope[0]], inc="s_dv")
                    r6 = b.emit("vector", lambda e, qstg=qstg: e.tensor_tensor(
                        out=qstg[64:128, 0:256], in0=rsc[0:64, 512:768],
                        in1=rsc[0:64, 768:1024], op=ALU.add),
                        deps=[r4, r5, qst_prev.get(oc % 2)], inc="s_dv")
                    last_rope[0] = r6
                    ps_free[oc % 2] = r5
                    qst = b.dma("scalar", qsp[oc, :, csl], qstg[:], deps=[r3, r6],
                                stream=f"qs{oc % 2}")
                    qst_prev[oc % 2] = qst
                    proj_toks.append(qst)
                c0 = hf * 512
                csl = slice(c0, c0 + 512)
                for oc in range(kvh):
                    wsb = wcol_s[oc % 2]
                    cts = load_wcol(wk_t, oc, oc % 2, lnr)
                    pps = pp[oc % 2]
                    mm = None
                    for i in range(nch):
                        mm = b.emit("tensor", lambda e, i=i, wsb=wsb, pps=pps: e.matmul(
                            pps[:], wsb[:, ts(i)], ln2T(i),
                            start=(i == 0), stop=(i == nch - 1)),
                            deps=[cts[i // hc], ps_free[oc % 2]] + lnr,
                            inc="s_pe" if i == nch - 1 else None)
                    wprev[oc % 2] = mm
                    k1 = b.emit("vector", lambda e, pps=pps, csl=csl: e.tensor_tensor(
                        out=rsc[0:64, 0:512], in0=pps[0:64, :],
                        in1=cosk_s[0:64, csl], op=ALU.mult),
                        deps=[mm, rope_tok, last_rope[0]], inc="s_dv")
                    k2 = b.emit("vector", lambda e, pps=pps, csl=csl: e.tensor_tensor(
                        out=rsc[0:64, 512:1024], in0=pps[64:128, :],
                        in1=sink_s[0:64, csl], op=ALU.mult),
                        deps=[mm, last_rope[0]], inc="s_dv")
                    k3 = b.emit("vector", lambda e, oc=oc, csl=csl: e.tensor_tensor(
                        out=kbuf[oc][0:64, csl], in0=rsc[0:64, 0:512],
                        in1=rsc[0:64, 512:1024], op=ALU.subtract),
                        deps=[k1, k2], inc="s_dv")
                    k4 = b.emit("vector", lambda e, pps=pps, csl=csl: e.tensor_tensor(
                        out=rsc[64:128, 0:512], in0=pps[64:128, :],
                        in1=cosk_s[64:128, csl], op=ALU.mult),
                        deps=[mm, last_rope[0]], inc="s_dv")
                    k5 = b.emit("vector", lambda e, pps=pps, csl=csl: e.tensor_tensor(
                        out=rsc[64:128, 512:1024], in0=pps[0:64, :],
                        in1=sink_s[64:128, csl], op=ALU.mult),
                        deps=[mm, last_rope[0]], inc="s_dv")
                    k6 = b.emit("vector", lambda e, oc=oc, csl=csl: e.tensor_tensor(
                        out=kbuf[oc][64:128, csl], in0=rsc[64:128, 0:512],
                        in1=rsc[64:128, 512:1024], op=ALU.add),
                        deps=[k4, k5], inc="s_dv")
                    last_rope[0] = k6
                    ps_free[oc % 2] = k6
                    proj_toks += [k3, k6]
                with ExitStack() as sv:
                    v_ps = [mkp(sv, f"v_ps{hf}_{m}", [128, VW]) for m in range(4)]
                    bank_last = {m: None for m in range(4)}
                    for og in range(cfg.kvd // VW):
                        mmv = None
                        for i in range(nch):
                            wsb = wcol_s[i % 2]
                            if i % 4 == 3:
                                wtok = b.dma("gpsimd", wsb[:, 0:VW],
                                             wv_t[ts(i), og * VW:(og + 1) * VW],
                                             deps=lnr + [wprev.get(i % 2)],
                                             stream="bwg")
                            else:
                                lt = b.dma("sync", wf32[i % 2][:, 0:VW],
                                           wv_t[ts(i), og * VW:(og + 1) * VW],
                                           deps=lnr + [wfprev.get(i % 2)],
                                           stream=f"bw{i % 2}")
                                wtok = b.emit("scalar", lambda e, i=i, wsb=wsb: e.copy(
                                    out=wsb[:, 0:VW], in_=wf32[i % 2][:, 0:VW]),
                                    deps=[lt, wprev.get(i % 2)], inc="s_ac")
                                wfprev[i % 2] = wtok
                            for m in range(4):
                                dep_m = [wtok] + lnr
                                if i == 0:
                                    dep_m.append(bank_last[m])
                                mmv = b.emit("tensor",
                                             lambda e, i=i, m=m, wsb=wsb: e.matmul(
                                                 v_ps[m][:], ln2T(i)[:, ts(m)],
                                                 wsb[:, 0:VW],
                                                 start=(i == 0), stop=(i == nch - 1)),
                                             deps=dep_m,
                                             inc="s_pe" if m == 3 else None)
                            wprev[i % 2] = mmv
                        for m in range(4):
                            tile = hf * 4 + m
                            tk = b.emit("scalar", lambda e, m=m, og=og, tile=tile: e.copy(
                                out=vbuf[tile][:, og * VW:(og + 1) * VW],
                                in_=v_ps[m][:]), deps=[mmv], inc="s_ac")
                            bank_last[m] = tk
                            proj_toks.append(tk)
        b.barrier(proj_toks)

        attn_toks = []
        with ExitStack() as sat:
            qcur2 = [mk(sat, f"qcur{i}", [128, nch * 64], BF16) for i in range(2)]
            s_sb = [[mk(sat, f"s_sb{gp}_{p}", [128, S], F32) for p in range(2)]
                    for gp in range(2)]
            p_sb = [[mk(sat, f"p_sb{gp}_{p}", [128, S], BF16) for p in range(2)]
                    for gp in range(2)]
            pt_sb = [mk(sat, f"pt_sb{gp}", [128, (S // 128) * 256], BF16)
                     for gp in range(2)]
            cm_sb = [mk(sat, f"cm_sb{gp}", [128, S // 16], F32) for gp in range(2)]
            m8_sb = [mk(sat, f"m8_sb{gp}", [128, 8], F32) for gp in range(2)]
            ta_sb = [mk(sat, f"ta_sb{gp}", [128, S // 16], F32) for gp in range(2)]
            mx_sb = [mk(sat, f"mx_sb{gp}", [128, 128], F32) for gp in range(2)]
            rs_sb = [mk(sat, f"rs_sb{gp}", [128, 8], F32) for gp in range(2)]
            mend_s = mk(sat, "mend_s", [128, 128], F32)
            mrec_s = mk(sat, "mrec_s", [128, 128], F32)
            avst = [mk(sat, f"avst{gp}", [128, 256], BF16) for gp in range(2)]
            sps = [[mkp(sat, f"s_ps{gp}_{p}", [128, 512]) for p in range(2)]
                   for gp in range(2)]
            ptp = [mkp(sat, f"ptp{i}", [128, 1024], BF16) for i in range(2)]
            av_ps = [mkp(sat, f"av_ps{gp}", [128, 256]) for gp in range(2)]

            mtok = [b.dma("sync", mend_s[:], m_end, stream="c0"),
                    b.dma("sync", mrec_s[:], m_rec, stream="c0")]

            iters = [(j, g) for j in range(cfg.qt) for g in range(kvh)]
            if cfg.attn_iters is not None:
                iters = iters[:cfg.attn_iters]
            qload_tok = {}
            last_score_q = {}
            ps_prev = [[None, None], [None, None]]
            sm_prev = [[None, None], [None, None]]
            ptk_prev = [[None, None], [None, None]]
            av_prev = [None, None]
            avc_prev = [None, None]
            evs = [None, None]
            store_prev = None

            for it, (j, g) in enumerate(iters):
                gp = it % 2
                L = 128 * (j + 1)
                nchk = L // 16
                if g == 0:
                    qload_tok[j] = b.dma(
                        "sync", qcur2[j % 2].ap().rearrange("p (c f) -> p c f", f=64),
                        qsp[:, :, ts(j, 64)].transpose([1, 0, 2]),
                        deps=proj_toks[-1:] + [last_score_q.get(j - 2)],
                        stream=f"ql{j % 2}")
                qc = qcur2[j % 2]
                ev_pair = []
                for p in range(2):
                    pps = sps[gp][p]
                    pane = s_sb[gp][p]
                    h0 = 4 * g + 2 * p
                    ev_toks = []
                    dtile = L - 128
                    for c0 in range(0, L, 512):
                        cw = min(512, L - c0)
                        mdep = [qload_tok[j], ps_prev[gp][p]]
                        m1 = b.emit("tensor", lambda e, h0=h0, c0=c0, cw=cw, pps=pps, g=g, qc=qc: e.matmul(
                            pps[0:64, 0:cw], qc[:, ts(h0, 64)],
                            kbuf[g][:, c0:c0 + cw], start=True, stop=True,
                            tile_position=(0, 0)), deps=mdep, inc=None)
                        m2 = b.emit("tensor", lambda e, h0=h0, c0=c0, cw=cw, pps=pps, g=g, qc=qc: e.matmul(
                            pps[64:128, 0:cw], qc[:, ts(h0 + 1, 64)],
                            kbuf[g][:, c0:c0 + cw], start=True, stop=True,
                            tile_position=(0, 64)), deps=mdep, inc="s_pe")
                        last_score_q[j] = m2
                        lo, hi = c0, c0 + cw
                        ev = None
                        if dtile > lo:
                            ev = b.emit("vector", lambda e, lo=lo, m=min(dtile, hi), pps=pps, pane=pane: e.tensor_copy(
                                out=pane[:, lo:m], in_=pps[:, 0:m - lo]),
                                deps=[m2, sm_prev[gp][p]], inc="s_dv")
                        if hi > dtile:
                            ev = b.emit("vector", lambda e, lo=lo, dtile=dtile, hi=hi, pps=pps, pane=pane: e.tensor_tensor(
                                out=pane[:, dtile:hi],
                                in0=pps[:, dtile - lo:hi - lo],
                                in1=mend_s[:, 0:hi - dtile], op=ALU.add),
                                deps=[m2, mtok[0], sm_prev[gp][p]] + ([ev] if ev else []),
                                inc="s_dv")
                        ps_prev[gp][p] = ev
                        ev_toks.append(ev)
                    ev_pair.append(ev_toks)
                nm_pair = []
                for p in range(2):
                    nm, extok = _softmax_pair(cfg, b, j, L, nchk, s_sb[gp][p],
                                              p_sb[gp][p], cm_sb[gp], m8_sb[gp],
                                              ta_sb[gp], mx_sb[gp], rs_sb[gp],
                                              mrec_s, mtok, ev_pair[p],
                                              ptk_prev[gp][p])
                    sm_prev[gp][p] = extok
                    nm_pair.append(nm)
                pair_ptk = []
                for p in range(2):
                    ptk = None
                    for kt in range(j + 1):
                        dep_t = [nm_pair[p], av_prev[gp], evs[kt % 2]] + list(consts)
                        pe_t = b.emit("tensor", lambda e, kt=kt, p=p, gp=gp: e.transpose(
                            ptp[kt % 2][:, 0:128], p_sb[gp][p][:, ts(kt)],
                            ident_s[:]), deps=dep_t, inc="s_pe")
                        ptk = b.emit("scalar", lambda e, kt=kt, p=p, gp=gp: e.copy(
                            out=pt_sb[gp][:, kt * 256 + p * 128:
                                           kt * 256 + p * 128 + 128],
                            in_=ptp[kt % 2][:, 0:128]), deps=[pe_t], inc="s_ac")
                        evs[kt % 2] = ptk
                    ptk_prev[gp][p] = ptk
                    pair_ptk.append(ptk)
                mmav = None
                for kt in range(j + 1):
                    mmav = b.emit("tensor", lambda e, kt=kt, g=g, j=j, gp=gp: e.matmul(
                        av_ps[gp][:], vbuf[kt][:, ts(g)],
                        pt_sb[gp][:, kt * 256: kt * 256 + 256],
                        start=(kt == 0), stop=(kt == j)),
                        deps=pair_ptk + [avc_prev[gp]],
                        inc="s_pe" if kt == j else None)
                av_prev[gp] = mmav
                avc = b.emit("scalar", lambda e, gp=gp: e.copy(
                    out=avst[gp][:], in_=av_ps[gp][:]),
                    deps=[mmav, store_prev], inc="s_ac")
                avc_prev[gp] = avc
                store_prev = b.dma("scalar",
                                   asp[4 * g: 4 * g + 4, :, ts(j, 64)].transpose([1, 0, 2]),
                                   avst[gp].ap().rearrange("p (c f) -> p c f", f=64),
                                   deps=[avc], stream="as0")
                attn_toks.append(store_prev)
        b.barrier(attn_toks)

    op_toks = []
    with ExitStack() as so:
        ao_t = mk(so, "ao_sb", [128, nch * 512], BF16)

        def ao_sb(ch):
            return ao_t[:, ch * 512:(ch + 1) * 512]
        wos = [mk(so, f"wos{i}", [128, 512], BF16) for i in range(2)]
        wof = [mk(so, f"wof{i}", [128, 512], F32) for i in range(2)]
        xo_sb = mk(so, "xo_sb", [128, 512], F32)
        op_ps = [mkp(so, f"op_ps{m}", [128, 512]) for m in range(4)]

        bank_last = {m: None for m in range(4)}
        wofprev = {}
        for mg in range(cfg.own // 512):
            ao_tok = b.dma("sync", ao_t.ap().rearrange("p (c f) -> p c f", f=512),
                           asp[:, :, ts(mg, 512)].transpose([1, 0, 2]),
                           deps=attn_toks[-1:] + [bank_last[m] for m in range(4)],
                           stream="ao")
            wlast = {0: None, 1: None}
            for og in range(nog):
                mmo = None
                for ho in range(nch):
                    wsb = wos[ho % 2]
                    if ho % 4 == 3:
                        wtok = b.dma("gpsimd", wsb[:], wo_t[ts(ho), ts(og, 512)],
                                     deps=[ao_tok, wlast[ho % 2]], stream="wog")
                    else:
                        lt = b.dma("sync", wof[ho % 2][:],
                                   wo_t[ts(ho), ts(og, 512)],
                                   deps=[ao_tok, wofprev.get(ho % 2)],
                                   stream=f"wo{ho % 2}")
                        wtok = b.emit("scalar", lambda e, ho=ho, wsb=wsb: e.copy(
                            out=wsb[:], in_=wof[ho % 2][:]),
                            deps=[lt, wlast[ho % 2]], inc="s_ac")
                        wofprev[ho % 2] = wtok
                    for m in range(4):
                        dep_m = [wtok, ao_tok]
                        if ho == 0:
                            dep_m.append(bank_last[m])
                        mmo = b.emit("tensor", lambda e, ho=ho, m=m, wsb=wsb: e.matmul(
                            op_ps[m][:], ao_sb(ho)[:, ts(m)], wsb[:],
                            start=(ho == 0), stop=(ho == nch - 1)),
                            deps=dep_m,
                            inc="s_pe" if m == 3 else None)
                    wlast[ho % 2] = mmo
                for m in range(4):
                    i_own = mg * 4 + m
                    g0, g1 = own_rows(xres, i_own, ts(og, 512))
                    b.dma("sync", xo_sb[0:64, :], g0, deps=[mmo] + op_toks[-1:],
                          stream="xo")
                    xl = b.dma("sync", xo_sb[64:128, :], g1,
                               deps=[mmo] + op_toks[-1:], stream="xo")
                    ad = b.emit("vector", lambda e, m=m: e.tensor_tensor(
                        out=xo_sb[:], in0=xo_sb[:], in1=op_ps[m][:], op=ALU.add),
                        deps=[xl, mmo], inc="s_dv")
                    bank_last[m] = ad
                    b.dma("scalar", g0, xo_sb[0:64, :], deps=[ad], stream="xs")
                    op_toks.append(b.dma("scalar", g1, xo_sb[64:128, :], deps=[ad],
                                         stream="xs"))
    b.barrier(op_toks)


def _softmax_pair(cfg, b, j, L, nchk, s_sb, p_sb, cm_sb, m8_sb, ta_sb,
                  mx_sb, rs_sb, mrec_s, mtok, ev_toks, p_war):
    cm = b.emit("vector", lambda e, L=L, nchk=nchk: e.tensor_reduce(
        out=cm_sb[:, 0:nchk],
        in_=s_sb.ap()[:, 0:L].rearrange("p (c k) -> p c k", k=16),
        op=ALU.max, axis=AX.X), deps=ev_toks, inc="s_dv")
    m8 = b.emit("vector", lambda e, nchk=nchk: e.max(
        out=m8_sb[:], in_=cm_sb[:, 0:nchk]), deps=[cm], inc="s_dv")
    ta = b.emit("vector", lambda e, nchk=nchk: e.tensor_scalar(
        out=ta_sb[:, 0:nchk], in0=cm_sb[:, 0:nchk], scalar1=m8_sb[:, 7:8],
        scalar2=NEGV, op0=ALU.is_lt, op1=ALU.mult), deps=[m8], inc="s_dv")
    deps_f = [ta]
    if j >= 2:
        if j >= 3:
            tav = ta_sb.ap()[:, 8:8 * (j - 1)].unsqueeze(2).broadcast_to(
                [128, 8 * (j - 1) - 8, 16])
            deps_f.append(b.emit("vector", lambda e, j=j, tav=tav: e.tensor_tensor(
                out=s_sb.ap()[:, 128:128 * (j - 1)].rearrange(
                    "p (c k) -> p c k", k=16),
                in0=s_sb.ap()[:, 128:128 * (j - 1)].rearrange(
                    "p (c k) -> p c k", k=16),
                in1=tav, op=ALU.add), deps=[ta], inc="s_dv"))
        tav2 = ta_sb.ap()[:, 8 * (j - 1):8 * j].unsqueeze(2).broadcast_to(
            [128, 8, 16])
        mx = b.emit("vector", lambda e, tav2=tav2: e.tensor_tensor(
            out=mx_sb.ap().rearrange("p (c k) -> p c k", k=16),
            in0=mrec_s.ap().rearrange("p (c k) -> p c k", k=16),
            in1=tav2, op=ALU.max), deps=[ta, mtok[1]], inc="s_dv")
        deps_f.append(b.emit("vector", lambda e, j=j: e.tensor_tensor(
            out=s_sb[:, 128 * (j - 1):128 * j],
            in0=s_sb[:, 128 * (j - 1):128 * j],
            in1=mx_sb[:], op=ALU.add), deps=[mx], inc="s_dv"))
    rm = b.emit("vector", lambda e, L=L: e.tensor_reduce(
        out=rs_sb[:, 0:1], in_=s_sb[:, 0:L], op=ALU.max, axis=AX.X,
        negate=True), deps=deps_f, inc="s_dv")
    ex = b.emit("scalar", lambda e, L=L: e.activation(
        out=p_sb[:, 0:L], in_=s_sb[:, 0:L], func=AF.Exp,
        bias=rs_sb[:, 0:1], accum_out=rs_sb[:, 1:2]),
        deps=[rm, p_war], inc="s_ac")
    rc = b.emit("vector", lambda e: e.reciprocal(
        out=rs_sb[:, 2:3], in_=rs_sb[:, 1:2]), deps=[ex], inc="s_dv")
    nm = b.emit("vector", lambda e, L=L: e.tensor_scalar(
        out=p_sb[:, 0:L], in0=p_sb[:, 0:L], scalar1=rs_sb[:, 2:3], scalar2=None,
        op0=ALU.mult), deps=[rc], inc="s_dv")
    return nm, ex


def _ffn_split(cfg, b, nc, stack, mk, mkp, ln_into, consts, *, biasb, wlt,
               lnTsp, src, dst, ntiles):
    d, nch, nog = cfg.d, cfg.nch, cfg.nog
    hog = nog // 2
    cols0 = hog * 512
    x_t = [mk(stack, f"fx_t{i}", [128, d], F32) for i in range(4)]
    ln_s = mk(stack, "fln_s", [128, d], BF16)
    lnT_t = mk(stack, "flnT", [128, nch * 512], BF16)
    w_sb = [mk(stack, f"fw_sb{i}", [128, 512], BF16) for i in range(2)]
    ev_s = mk(stack, "fev_s", [128, 512], F32)
    biasb_s = mk(stack, "fbias_s", [128, d], F32)
    ffn_ps = [mkp(stack, f"f_ps{m}", [128, 512]) for m in range(4)]
    ln_ps = [mkp(stack, f"fln_ps{i}", [128, 1024], BF16) for i in range(2)]

    bt = b.dma("sync", biasb_s[:], biasb, stream="c0")
    hb = b.emit("vector", lambda e: e.tensor_scalar(
        out=biasb_s[:], in0=biasb_s[:], scalar1=0.5, scalar2=None,
        op0=ALU.mult), deps=[bt], inc="s_dv")

    def lnT(ch):
        return lnT_t[:, ch * 512:(ch + 1) * 512]

    def lnT_view(ch, col):
        return lnT(ch)[:, ts(col)]

    st = dict(prev_half=[], lnr_guard=[], bank_guard={m: None for m in range(4)})

    def og_block(og, lnr, wlast, bank_last, deq_blk, res_toks):
        wdep = deq_blk[min(4 * og + 3, len(deq_blk) - 1)] if deq_blk else None
        mm_last = None
        for i in range(nch):
            wsb = w_sb[i % 2]
            wl_view = wlt[4 * og: 4 * og + 4, ts(i), :].transpose([1, 0, 2])
            wtok = b.dma("sync", wsb.ap().rearrange("p (bb f) -> p bb f", f=128),
                         wl_view, deps=[wdep, wlast[i % 2]] + lnr,
                         stream=f"fw{i % 2}")
            for m in range(4):
                dep_m = [wtok] + lnr
                if i == 0:
                    dep_m.append(bank_last[m] if bank_last[m] is not None
                                 else st["bank_guard"][m])
                mm_last = b.emit("tensor", lambda e, i=i, m=m, wsb=wsb: e.matmul(
                    ffn_ps[m][:], lnT(i)[:, ts(m)], wsb[:],
                    start=(i == 0), stop=(i == nch - 1)),
                    deps=dep_m, inc="s_pe" if m == 3 else None)
            wlast[i % 2] = mm_last
        for m in range(4):
            e1 = b.emit("vector", lambda e, m=m, og=og: e.scalar_tensor_tensor(
                out=ev_s[:], in0=ffn_ps[m][:], scalar=0.5,
                in1=biasb_s[:, ts(og, 512)], op0=ALU.mult, op1=ALU.add),
                deps=[mm_last, hb] + res_toks[-1:], inc="s_dv")
            e2 = b.emit("gpsimd", lambda e, m=m, og=og: e.tensor_tensor(
                out=x_t[m][:, ts(og, 512)], in0=x_t[m][:, ts(og, 512)],
                in1=ev_s[:], op=ALU.add), deps=[e1], inc="s_pl")
            bank_last[m] = e2
            res_toks.append(e2)
        return mm_last

    def pass0(deq_blk, after_og=None):
        slot = 0
        for hf in range(ntiles // 4):
            ln_toks = []
            res_toks = []
            for i in range(4):
                ld = b.dma("sync", x_t[i][:], src(hf * 4 + i),
                           deps=st["prev_half"] + ln_toks[-1:],
                           stream=f"fx{i % 2}")
                ln_toks.append(ln_into(x_t[i], ld, ln_s, ln_ps, lnT_view, i,
                                       deps=ln_toks[-1:] + st["lnr_guard"]))
            lnr = [ln_toks[-1]]
            wlast = {0: None, 1: None}
            bank_last = {m: None for m in range(4)}
            for og in range(hog):
                og_block(og, lnr, wlast, bank_last, deq_blk, res_toks)
                if after_og is not None:
                    after_og(slot)
                slot += 1
            sp = b.dma("scalar", lnTsp[hf], lnT_t[:], deps=lnr, stream="fsp")
            st["lnr_guard"] = [wlast[0], wlast[1], sp, res_toks[-1]]
            st["bank_guard"] = dict(bank_last)
            sts = []
            for i in range(4):
                sts.append(b.dma("scalar", dst(hf * 4 + i)[:, 0:cols0],
                                 x_t[i][:, 0:cols0], deps=res_toks,
                                 stream=f"fs{i % 2}"))
            st["prev_half"] = sts

    def pass1(deq_blk):
        all_st = []
        for hf in range(ntiles // 4):
            rl = b.dma("sync", lnT_t[:], lnTsp[hf], deps=st["lnr_guard"],
                       stream="frl")
            lnr = [rl]
            ld_last = [None, None]
            for i in range(4):
                ld_last[i % 2] = b.dma("sync", x_t[i][:, cols0:d],
                                       src(hf * 4 + i)[:, cols0:d],
                                       deps=st["prev_half"], stream=f"fx{i % 2}")
            lnr += ld_last
            res_toks = []
            wlast = {0: None, 1: None}
            bank_last = {m: None for m in range(4)}
            for og in range(hog, nog):
                og_block(og, lnr, wlast, bank_last, deq_blk, res_toks)
            st["lnr_guard"] = [wlast[0], wlast[1], res_toks[-1]]
            st["bank_guard"] = dict(bank_last)
            sts = []
            for i in range(4):
                sts.append(b.dma("scalar", dst(hf * 4 + i)[:, cols0:d],
                                 x_t[i][:, cols0:d], deps=res_toks,
                                 stream=f"fs{i % 2}"))
            st["prev_half"] = sts
            all_st += sts
        return all_st

    return pass0, pass1


def _ffn_sublayer(cfg, b, nc, stack, mk, mkp, ln_into, consts, biasb, wlt,
                  deq_blk, src_tile, dst_tile, ntiles, pre="f"):
    d, nch, nog = cfg.d, cfg.nch, cfg.nog
    x_t = [mk(stack, pre + f"x_t{i}", [128, d], F32) for i in range(4)]
    ln_s = mk(stack, pre + "ln_s", [128, d], BF16)
    lnT_t = mk(stack, pre + "lnT", [128, nch * 512], BF16)

    def lnT(ch):
        return lnT_t[:, ch * 512:(ch + 1) * 512]
    w_sb = [mk(stack, pre + f"w_sb{i}", [128, 512], BF16) for i in range(2)]
    ev_s = mk(stack, pre + "ev_s", [128, 512], F32)
    biasb_s = mk(stack, pre + "bias_s", [128, d], F32)
    ffn_ps = [mkp(stack, pre + f"_ps{m}", [128, 512]) for m in range(4)]
    ln_ps = [mkp(stack, pre + f"ln_ps{i}", [128, 1024], BF16) for i in range(2)]

    bt = b.dma("sync", biasb_s[:], biasb, stream="c0")
    hb = b.emit("vector", lambda e: e.tensor_scalar(
        out=biasb_s[:], in0=biasb_s[:], scalar1=0.5, scalar2=None,
        op0=ALU.mult), deps=[bt], inc="s_dv")

    def lnT_view(ch, col):
        return lnT(ch)[:, ts(col)]

    all_st = []
    prev_half = []
    for hf in range(ntiles // 4):
        ln_toks = []
        ld_toks = []
        for i in range(4):
            stile = src_tile(hf * 4 + i)
            if isinstance(stile, list):
                b.dma("sync", x_t[i][0:64, :], stile[0],
                      deps=prev_half + ln_toks[-1:], stream=pre + f"x{i}")
                ld = b.dma("sync", x_t[i][64:128, :], stile[1],
                           deps=prev_half + ln_toks[-1:], stream=pre + f"x{i}")
            else:
                ld = b.dma("sync", x_t[i][:], stile,
                           deps=prev_half + ln_toks[-1:], stream=pre + f"x{i}")
            ld_toks.append(ld)
            ln_toks.append(ln_into(x_t[i], ld, ln_s, ln_ps, lnT_view, i,
                                   deps=ln_toks[-1:]))
        lnr = [ln_toks[-1]]
        res_toks = []
        bank_last = {m: None for m in range(4)}
        wlast = {0: None, 1: None}
        for og in range(nog):
            wdep = deq_blk[min(4 * og + 3, len(deq_blk) - 1)] if deq_blk else None
            mm_last = None
            for i in range(nch):
                wsb = w_sb[i % 2]
                wl_view = wlt[4 * og: 4 * og + 4, ts(i), :].transpose([1, 0, 2])
                wtok = b.dma("sync", wsb.ap().rearrange("p (b f) -> p b f", f=128),
                             wl_view,
                             deps=[wdep, wlast[i % 2]] + lnr, stream=pre + f"w{i % 2}")
                for m in range(4):
                    dep_m = [wtok] + lnr
                    if i == 0 and bank_last[m] is not None:
                        dep_m.append(bank_last[m])
                    mm_last = b.emit("tensor", lambda e, i=i, m=m, wsb=wsb: e.matmul(
                        ffn_ps[m][:], lnT(i)[:, ts(m)], wsb[:],
                        start=(i == 0), stop=(i == nch - 1)),
                        deps=dep_m,
                        inc="s_pe" if m == 3 else None)
                wlast[i % 2] = mm_last
            for m in range(4):
                e1 = b.emit("vector", lambda e, m=m, og=og: e.scalar_tensor_tensor(
                    out=ev_s[:], in0=ffn_ps[m][:], scalar=0.5,
                    in1=biasb_s[:, ts(og, 512)], op0=ALU.mult, op1=ALU.add),
                    deps=[mm_last, hb] + res_toks[-1:], inc="s_dv")
                e2 = b.emit("gpsimd", lambda e, m=m, og=og: e.tensor_tensor(
                    out=x_t[m][:, ts(og, 512)], in0=x_t[m][:, ts(og, 512)],
                    in1=ev_s[:], op=ALU.add), deps=[e1], inc="s_pl")
                bank_last[m] = e2
                res_toks.append(e2)
        st_toks = []
        for i in range(4):
            dtile = dst_tile(hf * 4 + i)
            if isinstance(dtile, list):
                b.dma("scalar", dtile[0], x_t[i][0:64, :], deps=res_toks,
                      stream=pre + f"s{i % 2}")
                st_toks.append(b.dma("scalar", dtile[1], x_t[i][64:128, :],
                                     deps=res_toks, stream=pre + f"s{i % 2}"))
            else:
                st_toks.append(b.dma("scalar", dtile, x_t[i][:], deps=res_toks,
                                     stream=pre + f"s{i % 2}"))
        all_st += st_toks
        prev_half = st_toks
    return all_st


def shard_inputs(cfg, x_full, qweight, lut4, bias, wq, wk, wv, wo):
    wq_t = np.ascontiguousarray(wq.T)
    wk_t = np.ascontiguousarray(wk.T)
    wv_t = np.ascontiguousarray(wv.T)
    wo_t = np.ascontiguousarray(wo.T)
    qw4 = np.ascontiguousarray(qweight[0:4])
    biasb = np.broadcast_to(bias[None, :], (128, cfg.d)).copy()
    in_maps = []
    for c in range(8):
        bb, half = c // 2, c % 2
        t = make_tables(cfg, half)
        in_maps.append(dict(
            xin=permute_seq(x_full[bb], half),
            qw4=qw4, lut4=np.ascontiguousarray(lut4), biasb=biasb,
            wq_t=wq_t, wk_t=wk_t, wv_t=wv_t, wo_t=wo_t,
            mask32=t["mask32"], ident=t["ident"],
            cos_k=t["cos_k"], sin_k=t["sin_k"],
            cos_q=t["cos_q"], sin_q=t["sin_q"],
            m_end=t["m_end"], m_rec=t["m_rec"]))
    return in_maps


def assemble_output(cfg, results):
    out = np.zeros((4, cfg.seq, cfg.d), dtype=np.float32)
    o = np.arange(cfg.own)
    for c in range(8):
        bb, half = c // 2, c % 2
        rows = np.asarray(results[c]["xout"])
        out[bb, 128 * (o // 64) + 64 * half + (o % 64), :] = rows
    return out


def kernel(L0_attn_hidden_states, L0_attn_wq, L0_attn_wk, L0_attn_wv, L0_attn_wo,
           L0_ffn_qweight, L0_ffn_lut4, L0_ffn_lut8, L0_ffn_bias):
    cfg = Cfg()
    nc = build_nc(cfg)
    in_maps = shard_inputs(cfg, np.asarray(L0_attn_hidden_states),
                           np.asarray(L0_ffn_qweight), np.asarray(L0_ffn_lut4),
                           np.asarray(L0_ffn_bias), np.asarray(L0_attn_wq),
                           np.asarray(L0_attn_wk), np.asarray(L0_attn_wv),
                           np.asarray(L0_attn_wo))
    res = run_bass_kernel_spmd(nc, in_maps, core_ids=list(range(8)))
    return assemble_output(cfg, res.results)


# version 12
# speedup vs baseline: 1.2515x; 1.0735x over previous
import math
import numpy as np
import ml_dtypes
import concourse.bass as bass
import concourse.mybir as mybir
from concourse.bass_utils import run_bass_kernel_spmd

F32 = mybir.dt.float32
BF16 = mybir.dt.bfloat16
I32 = mybir.dt.int32
AF = mybir.ActivationFunctionType
ALU = mybir.AluOpType
AX = mybir.AxisListType

NEGV = -1.0e30
EPS = 1e-5


def ts(i, n=128):
    return slice(i * n, (i + 1) * n)


class Cfg:
    def __init__(self, d=4096, seq=2048, sub_b=True, sub_c=True,
                 b_stop="full", dbg=False, attn_iters=None):
        self.b_stop = b_stop
        self.dbg = dbg
        self.attn_iters = attn_iters
        self.d = d
        self.seq = seq
        self.kvd = d // 4
        self.heads = d // 128
        self.kvheads = self.kvd // 128
        self.words = d // 32
        self.nblk = d // 128
        self.nch = d // 128
        self.qt = seq // 128
        self.own = seq // 2
        self.nog = d // 512
        self.sub_b = sub_b
        self.sub_c = sub_c
        assert d % 512 == 0 and seq % 1024 == 0


class B:
    ENGS = ("sync", "scalar", "vector", "tensor", "gpsimd")

    def __init__(self, nc):
        self.nc = nc
        self.streams = {k: [] for k in self.ENGS}
        self.sems = {}
        self.counts = {}
        self.waited = {k: {} for k in self.ENGS}
        self.pending = {k: [] for k in self.ENGS}
        self.lastdma = {}
        self.semfactory = None

    def sem(self, name):
        self.counts[name] = 0

    def barrier(self, toks):
        toks = [t for t in toks if t is not None]
        for k in self.ENGS:
            self.pending[k] = list(toks)

    def emit(self, eng, fn, deps=(), inc=None, dinc=None):
        deps = list(deps) + self.pending[eng]
        self.pending[eng] = []
        waits = []
        for t in deps:
            if t is None:
                continue
            s, v = t
            if v <= 0:
                continue
            w = self.waited[eng]
            if w.get(s, -1) >= v:
                continue
            w[s] = v
            waits.append((s, v))
        out_tok = None
        inc_s, amt = None, 0
        if inc is not None:
            self.counts[inc] += 1
            out_tok = (inc, self.counts[inc])
            inc_s, amt = inc, 1
        elif dinc is not None:
            self.counts[dinc] += 16
            out_tok = (dinc, self.counts[dinc])
            inc_s, amt = dinc, 16
        self.streams[eng].append((fn, waits, inc_s, amt))
        return out_tok

    def dma(self, queue, out, in_, deps=(), stream="misc", **kw):
        sem = "d_" + stream
        if sem not in self.counts:
            self.counts[sem] = 0
            self.sems[sem] = self.semfactory(sem)
            self.lastdma[sem] = None
        deps = list(deps) + [self.lastdma[sem]]
        tok = self.emit(queue, lambda e: e.dma_start(out=out, in_=in_, **kw),
                        deps=deps, dinc=sem)
        self.lastdma[sem] = tok
        return tok

    def build(self, block):
        semobjs = self.sems

        def run(eng_name, engine):
            for (fn, waits, inc_s, amt) in self.streams[eng_name]:
                for (s, v) in waits:
                    engine.wait_ge(semobjs[s], v)
                ins = fn(engine)
                if inc_s is not None:
                    ins.then_inc(semobjs[inc_s], amt)

        @block.sync
        def _(e):
            run("sync", e)

        @block.scalar
        def _(e):
            run("scalar", e)

        @block.vector
        def _(e):
            run("vector", e)

        @block.tensor
        def _(e):
            run("tensor", e)

        @block.gpsimd
        def _(e):
            run("gpsimd", e)


def make_tables(cfg, half):
    t = {}
    t["mask32"] = np.broadcast_to((np.int32(1) << np.arange(32, dtype=np.int32)),
                                  (128, 32)).copy()
    t["ident"] = np.eye(128, dtype=ml_dtypes.bfloat16)
    invf = 1.0 / (10000.0 ** (np.arange(0, 128, 2, dtype=np.float64) / 128.0))
    invf_ext = np.concatenate([invf, invf])
    u = np.arange(128)
    pb = (u + 64 * half) % 128
    blocks = np.arange(cfg.seq // 128)
    seq_of_stored = (128 * blocks[:, None] + pb[None, :]).reshape(-1)
    ang = invf_ext[:, None] * seq_of_stored[None, :].astype(np.float64)
    t["cos_k"] = np.cos(ang).astype(np.float32)
    t["sin_k"] = np.sin(ang).astype(np.float32)
    o = np.arange(cfg.own)
    own_seq = 128 * (o // 64) + 64 * half + (o % 64)
    ang_q = invf_ext[:, None] * own_seq[None, :].astype(np.float64)
    sc = 1.0 / math.sqrt(128.0)
    t["cos_q"] = (np.cos(ang_q) * sc).astype(np.float32)
    t["sin_q"] = (np.sin(ang_q) * sc).astype(np.float32)
    r = np.arange(64)
    qpos_in_blk = 64 * half + r
    m = np.where(pb[None, :] <= qpos_in_blk[:, None], 0.0, NEGV).astype(np.float32)
    t["m_end"] = np.concatenate([m, m], axis=0)
    mr = np.where(pb[None, :] > qpos_in_blk[:, None], 0.0, NEGV).astype(np.float32)
    t["m_rec"] = np.concatenate([mr, mr], axis=0)
    return t


def permute_seq(x, half):
    if half == 0:
        return np.ascontiguousarray(x)
    S, d = x.shape
    v = x.reshape(S // 128, 2, 64, d)
    return np.ascontiguousarray(v[:, ::-1].reshape(S, d))


def build_nc(cfg):
    nc = bass.Bass("TRN2", target_bir_lowering=False)
    d, S = cfg.d, cfg.seq
    nch, nog, kvh = cfg.nch, cfg.nog, cfg.kvheads

    xin = nc.dram_tensor("xin", [S, d], F32, kind="ExternalInput").ap()
    qw4 = nc.dram_tensor("qw4", [4, d, cfg.words], I32, kind="ExternalInput").ap()
    lut4 = nc.dram_tensor("lut4", [d, 16], F32, kind="ExternalInput").ap()
    biasb = nc.dram_tensor("biasb", [128, d], F32, kind="ExternalInput").ap()
    wq_t = nc.dram_tensor("wq_t", [d, d], F32, kind="ExternalInput").ap()
    wk_t = nc.dram_tensor("wk_t", [d, cfg.kvd], F32, kind="ExternalInput").ap()
    wv_t = nc.dram_tensor("wv_t", [d, cfg.kvd], F32, kind="ExternalInput").ap()
    wo_t = nc.dram_tensor("wo_t", [d, d], F32, kind="ExternalInput").ap()
    mask32 = nc.dram_tensor("mask32", [128, 32], I32, kind="ExternalInput").ap()
    ident_in = nc.dram_tensor("ident", [128, 128], BF16, kind="ExternalInput").ap()
    cos_k = nc.dram_tensor("cos_k", [128, S], F32, kind="ExternalInput").ap()
    sin_k = nc.dram_tensor("sin_k", [128, S], F32, kind="ExternalInput").ap()
    cos_q = nc.dram_tensor("cos_q", [128, cfg.own], F32, kind="ExternalInput").ap()
    sin_q = nc.dram_tensor("sin_q", [128, cfg.own], F32, kind="ExternalInput").ap()
    m_end = nc.dram_tensor("m_end", [128, 128], F32, kind="ExternalInput").ap()
    m_rec = nc.dram_tensor("m_rec", [128, 128], F32, kind="ExternalInput").ap()

    xout = nc.dram_tensor("xout", [cfg.own, d], F32, kind="ExternalOutput").ap()

    wlt = nc.dram_tensor("wlt", [cfg.nblk, d, 128], BF16).ap()
    lnTsp = nc.dram_tensor("lnTsp", [S // 512, 128, nch * 512], BF16).ap()
    wqb = nc.dram_tensor("wqb", [d, d], BF16).ap()
    wkb = nc.dram_tensor("wkb", [d, cfg.kvd], BF16).ap()
    wvb = nc.dram_tensor("wvb", [d, cfg.kvd], BF16).ap()
    xres = nc.dram_tensor("xres", [S, d], F32).ap()
    qsp = nc.dram_tensor("qsp", [nch, 128, cfg.own], BF16).ap()
    asp = nc.dram_tensor("asp", [nch, 128, cfg.own], BF16).ap()

    def own_rows(src, i, cols=None):
        v = src.rearrange("(a r) d -> a r d", r=64)
        if cols is not None:
            v = v[:, :, cols]
        return [v[4 * i + 2 * a] for a in range(2)]

    def dma_own(queue, sbuf_tile, src, i, cols=None, write=False, deps=(),
                stream="misc"):
        g0, g1 = own_rows(src, i, cols)
        if write:
            b.dma(queue, g0, sbuf_tile[0:64, :], deps=deps, stream=stream)
            return b.dma(queue, g1, sbuf_tile[64:128, :], deps=deps, stream=stream)
        b.dma(queue, sbuf_tile[0:64, :], g0, deps=deps, stream=stream)
        return b.dma(queue, sbuf_tile[64:128, :], g1, deps=deps, stream=stream)

    b = B(nc)
    for s in ("s_pe", "s_dv", "s_ac", "s_pl"):
        b.sem(s)

    from contextlib import ExitStack
    with ExitStack() as root:
        def mk(stack, name, shape, dt):
            return stack.enter_context(nc.sbuf_tensor(name, shape, dt))

        def mkp(stack, name, shape, dt=F32):
            return stack.enter_context(nc.psum_tensor(name, shape, dt))

        for s in list(b.counts):
            b.sems[s] = root.enter_context(nc.semaphore(s))
        b.semfactory = lambda name: root.enter_context(nc.semaphore(name))
        block = root.enter_context(nc.Block())

        mask32_s = mk(root, "mask32_s", [128, 32], I32)
        ident_s = mk(root, "ident_s", [128, 128], BF16)
        stat_s = mk(root, "stat_s", [128, 64], F32)
        mv_s = mk(root, "mv_s", [128, 8], F32)
        eps_s = mk(root, "eps_s", [128, 1], F32)

        c_toks = [b.dma("sync", mask32_s[:], mask32, stream="c0"),
                  b.dma("sync", ident_s[:], ident_in, stream="c0"),
                  b.emit("vector", lambda e: e.memset(eps_s[:], EPS), inc="s_dv")]
        consts = tuple(c_toks)

        def ln_into(xt_tile, ld_tok, ln_s, ln_ps, lnT_view, col, deps=()):
            nst = d // 512
            st_toks = []
            for k in range(nst):
                st_toks.append(b.emit("vector", lambda e, k=k: e.bn_stats(
                    out=stat_s[:, 6 * k:6 * (k + 1)], in_=xt_tile[:, ts(k, 512)]),
                    deps=[ld_tok] + list(deps), inc="s_dv"))
            ag = b.emit("vector", lambda e, n=nst: e.bn_aggr(
                out=mv_s[:, 0:2], in_=stat_s[:, 0:6 * n]), deps=st_toks, inc="s_dv")
            l1 = b.emit("scalar", lambda e: e.activation(
                out=mv_s[:, 2:3], in_=mv_s[:, 1:2], func=AF.Ln, bias=eps_s[:]),
                deps=[ag] + list(consts), inc="s_ac")
            l2 = b.emit("scalar", lambda e: e.activation(
                out=mv_s[:, 3:4], in_=mv_s[:, 2:3], func=AF.Exp, scale=-0.5),
                deps=[l1], inc="s_ac")
            ap_tok = b.emit("vector", lambda e: e.tensor_scalar(
                out=ln_s[:], in0=xt_tile[:], scalar1=mv_s[:, 0:1],
                scalar2=mv_s[:, 3:4], op0=ALU.subtract, op1=ALU.mult),
                deps=[l2], inc="s_dv")
            evh = []
            for ch in range(nch):
                dep_t = [ap_tok] + list(consts)
                if len(evh) >= 2:
                    dep_t.append(evh[-2])
                pe_tok = b.emit("tensor", lambda e, ch=ch: e.transpose(
                    ln_ps[ch % 2][:, 0:128], ln_s[:, ts(ch)], ident_s[:]),
                    deps=dep_t, inc="s_pe")
                evh.append(b.emit("scalar", lambda e, ch=ch: e.copy(
                    out=lnT_view(ch, col), in_=ln_ps[ch % 2][:, 0:128]),
                    deps=[pe_tok], inc="s_ac"))
            return evh[-1]

        deq_blk = []
        with ExitStack() as sa:
            qw_s2 = [mk(sa, f"qw_s{i}", [128, 4 * cfg.words], I32) for i in range(2)]
            lut_s2 = [mk(sa, f"lut_s{i}", [128, 16], F32) for i in range(2)]
            idx_s2 = [mk(sa, f"idx_s{i}", [128, d], BF16) for i in range(2)]
            sgn_s2 = [mk(sa, f"sgn_s{i}", [128, d], BF16) for i in range(2)]
            and2 = [mk(sa, f"and_s{i}", [128, d // 2], I32) for i in range(2)]
            acc_s = mk(sa, "acc_s", [128, d], BF16)
            wtst_s = mk(sa, "wtst_s", [128, d], BF16)
            tp_ps2 = [mkp(sa, f"tp_ps{i}", [128, 1024], BF16) for i in range(2)]

            mask_view2 = mask32_s.ap().unsqueeze(1).broadcast_to(
                [128, cfg.words // 2, 32])

            dstate = dict(last_tp=None, sgn_by_buf={0: None, 1: None})
            prev_sel = [None, None]
            prev_idx = [[], []]

            pcb = [mk(sa, f"pcb{i}", [128, 2048], BF16) for i in range(2)]
            pc_jobs = []
            for w_src, w_dst, cols in ((wq_t, wqb, d), (wk_t, wkb, cfg.kvd),
                                       (wv_t, wvb, cfg.kvd)):
                w = min(2048, cols)
                for rb in range(d // 128):
                    for cb in range(cols // w):
                        pc_jobs.append((w_src, w_dst, rb, cb, w))
            pc_state = dict(i=0, prev={0: None, 1: None}, last=None)

            def emit_pc(nj):
                while nj > 0 and pc_state["i"] < len(pc_jobs):
                    w_src, w_dst, rb, cb, w = pc_jobs[pc_state["i"]]
                    j = pc_state["i"] % 2
                    lt = b.dma("gpsimd", pcb[j][:, 0:w],
                               w_src[ts(rb), cb * w:(cb + 1) * w],
                               deps=[pc_state["prev"][j]], stream=f"pc{j}")
                    stt = b.dma("sync", w_dst[ts(rb), cb * w:(cb + 1) * w],
                                pcb[j][:, 0:w], deps=[lt], stream=f"pcs{j}")
                    pc_state["prev"][j] = stt
                    pc_state["last"] = stt
                    pc_state["i"] += 1
                    nj -= 1

            dloads = {}

            def emit_load(blk):
                if blk in dloads or blk >= cfg.nblk:
                    return
                par = blk % 2
                qtok = b.dma("scalar",
                             qw_s2[par].ap().rearrange("p (a w) -> p a w", a=4),
                             qw4[:, ts(blk), :].transpose([1, 0, 2]),
                             deps=list(consts) + prev_idx[par],
                             stream=f"qw{par}")
                ltok = b.dma("scalar", lut_s2[par][:], lut4[ts(blk), :],
                             deps=list(consts) + [prev_sel[par]],
                             stream=f"lut{par}")
                dloads[blk] = (qtok, ltok)

            def emit_deq(blk):
                par = blk % 2
                qws, luts, idxs, sgns = (qw_s2[par], lut_s2[par],
                                         idx_s2[par], sgn_s2[par])
                emit_load(blk)
                qtok, ltok = dloads.pop(blk)
                emit_load(blk + 1)
                hwd = d // 2
                hwords = cfg.words // 2
                idx_half = [None, None]
                sign_of = {}

                def emit_and(s):
                    p, h = s // 2, s % 2
                    s_i = s % 2
                    ab = and2[s_i]
                    w0 = p * cfg.words + h * hwords
                    qv = qws.ap()[:, w0:w0 + hwords].unsqueeze(2).broadcast_to(
                        [128, hwords, 32])
                    a_tok = b.emit("vector", lambda e, qv=qv, ab=ab: e.tensor_tensor(
                        out=ab.ap().rearrange("p (w t) -> p w t", t=32),
                        in0=qv, in1=mask_view2, op=ALU.bitwise_and),
                        deps=[qtok, dstate["sgn_by_buf"][s_i]] + list(consts),
                        inc="s_dv")
                    dst_t = idxs if p == 0 else sgns
                    s_tok = b.emit("scalar", lambda e, ab=ab, dst_t=dst_t, h=h: e.activation(
                        out=dst_t[:, h * hwd:(h + 1) * hwd], in_=ab[:],
                        func=AF.Sign),
                        deps=[a_tok, prev_sel[par]], inc="s_ac")
                    dstate["sgn_by_buf"][s_i] = s_tok
                    sign_of[s] = s_tok
                    if p == 0:
                        idx_half[h] = s_tok

                def emit_stt(s):
                    p, h = s // 2, s % 2
                    idx_half[h] = b.emit("vector", lambda e, p=p, h=h, idxs=idxs, sgns=sgns: e.scalar_tensor_tensor(
                        out=idxs[:, h * hwd:(h + 1) * hwd],
                        in0=sgns[:, h * hwd:(h + 1) * hwd],
                        scalar=float(1 << p),
                        in1=idxs[:, h * hwd:(h + 1) * hwd],
                        op0=ALU.mult, op1=ALU.add),
                        deps=[sign_of[s], idx_half[h]], inc="s_dv")

                for step in (('A', 0), ('A', 1), ('A', 2), ('A', 3), ('T', 2),
                             ('A', 4), ('T', 3), ('A', 5), ('T', 4), ('A', 6),
                             ('T', 5), ('A', 7), ('T', 6), ('T', 7)):
                    (emit_and if step[0] == 'A' else emit_stt)(step[1])
                prev_idx[par] = list(idx_half)
                idx_tok = idx_half
                acc_tok = b.emit("vector", lambda e, idxs=idxs, luts=luts: e.tensor_scalar(
                    out=acc_s[:], in0=idxs[:], scalar1=0.0,
                    scalar2=luts[:, 0:1], op0=ALU.is_equal, op1=ALU.mult),
                    deps=idx_tok + [ltok, dstate["last_tp"]], inc="s_dv")
                for c in range(1, 16):
                    t_tok = b.emit("vector", lambda e, c=c, idxs=idxs, luts=luts, sgns=sgns: e.tensor_scalar(
                        out=sgns[:], in0=idxs[:],
                        scalar1=float(c), scalar2=luts[:, c:c + 1],
                        op0=ALU.is_equal, op1=ALU.mult),
                        deps=idx_tok + [ltok, acc_tok], inc="s_dv")
                    acc_tok = b.emit("vector", lambda e, sgns=sgns: e.tensor_tensor(
                        out=acc_s[:], in0=acc_s[:], in1=sgns[:], op=ALU.add),
                        deps=[t_tok], inc="s_dv")
                prev_sel[par] = acc_tok
                evt_hist = []
                for ch in range(nch):
                    dep_t = [acc_tok] + list(consts)
                    if ch == 0 and deq_blk:
                        dep_t.append(deq_blk[-1])
                    if len(evt_hist) >= 2:
                        dep_t.append(evt_hist[-2])
                    pe_tok = b.emit("tensor", lambda e, ch=ch: e.transpose(
                        tp_ps2[ch % 2][:, 0:128], acc_s[:, ts(ch)], ident_s[:]),
                        deps=dep_t, inc="s_pe")
                    evt_hist.append(b.emit("scalar", lambda e, ch=ch: e.copy(
                        out=wtst_s[:, ts(ch)], in_=tp_ps2[ch % 2][:, 0:128]),
                        deps=[pe_tok], inc="s_ac"))
                dstate["last_tp"] = pe_tok
                st = b.dma("scalar",
                           wlt[blk].rearrange("(c p) f -> c p f", p=128).transpose([1, 0, 2]),
                           wtst_s.ap().rearrange("p (c f) -> p c f", f=128),
                           deps=[evt_hist[-1]], stream="dst")
                deq_blk.append(st)

            ffn_p0, ffn_p1 = _ffn_split(
                cfg, b, nc, sa, mk, mkp, ln_into, consts,
                biasb=biasb, wlt=wlt, lnTsp=lnTsp,
                src=lambda i: xin[ts(i), :],
                dst=lambda i: xres[ts(i), :], ntiles=S // 128)

            H = cfg.nblk // 2
            emit_load(0)
            for blk in range(H):
                emit_deq(blk)
                emit_pc(3)
            rest = list(range(H, cfg.nblk))
            n_slots = (S // 512) * (cfg.nog // 2)
            di = [0]

            def after_og(slot):
                want = min((slot + 1) * len(rest) // n_slots, len(rest))
                while di[0] < want:
                    emit_deq(rest[di[0]])
                    di[0] += 1
                emit_pc((len(pc_jobs) + n_slots - 1) // n_slots)
            ffn_p0(deq_blk, after_og)
            while di[0] < len(rest):
                emit_deq(rest[di[0]])
                di[0] += 1
            emit_pc(len(pc_jobs))
            ffn_toks = ffn_p1(deq_blk)
            ffn_toks = ffn_toks + [pc_state["prev"][0], pc_state["prev"][1]]
        b.barrier(ffn_toks + [deq_blk[-1]])

        final_deps = ffn_toks

        if cfg.sub_b:
            _sublayer_b(cfg, b, nc, mk, mkp, ln_into, consts,
                        xres=xres, qsp=qsp, asp=asp, own_rows=own_rows,
                        wq_t=wqb, wk_t=wkb, wv_t=wvb, wo_t=wo_t,
                        cos_q=cos_q, sin_q=sin_q, cos_k=cos_k, sin_k=sin_k,
                        m_end=m_end, m_rec=m_rec, ident_s=ident_s,
                        final_deps=final_deps)
            final_deps = []

        if cfg.sub_c:
            with ExitStack() as sc:
                ffn2 = _ffn_sublayer(
                    cfg, b, nc, sc, mk, mkp, ln_into, consts,
                    biasb=biasb, wlt=wlt, deq_blk=deq_blk,
                    src_tile=lambda i: own_rows(xres, i),
                    dst_tile=lambda i: xout[ts(i), :],
                    ntiles=cfg.own // 128, pre="c")
            final_deps = ffn2
        else:
            cp = []
            if True:
                cp_t = mk(root, "cp_t", [128, d], F32)
                for i in range(cfg.own // 128):
                    t1 = dma_own("sync", cp_t, xres, i,
                                 deps=final_deps + cp[-1:], stream="cp")
                    cp.append(b.dma("scalar", xout[ts(i), :], cp_t[:], deps=[t1],
                                    stream="cps"))
            final_deps = cp

        b.emit("scalar", lambda e: e.nop(), deps=final_deps)

        b.build(block)
    return nc


def _sublayer_b(cfg, b, nc, mk, mkp, ln_into, consts, *, xres, qsp, asp,
                own_rows, wq_t, wk_t, wv_t, wo_t, cos_q, sin_q, cos_k, sin_k,
                m_end, m_rec, ident_s, final_deps):
    from contextlib import ExitStack
    d, S, nch, nog, kvh = cfg.d, cfg.seq, cfg.nch, cfg.nog, cfg.kvheads
    VW = min(512, cfg.kvd)

    with ExitStack() as skv:
        kbuf = [mk(skv, f"kbuf{g}", [128, S], BF16) for g in range(kvh)]
        vbuf = [mk(skv, f"vbuf{t}", [128, cfg.kvd], BF16) for t in range(S // 128)]
        proj_toks = []
        with ExitStack() as sp:
            xb_t = [mk(sp, f"xb_t{i}", [128, d], F32) for i in range(2)]
            ln_s = mk(sp, "ln2_s", [128, d], BF16)
            ln2T_t = mk(sp, "ln2T", [128, nch * 512], BF16)

            def ln2T(ch):
                return ln2T_t[:, ch * 512:(ch + 1) * 512]
            cosq_s = mk(sp, "cosq_s", [128, cfg.own], F32)
            sinq_s = mk(sp, "sinq_s", [128, cfg.own], F32)
            cosk_s = mk(sp, "cosk_s", [128, S], F32)
            sink_s = mk(sp, "sink_s", [128, S], F32)
            wcol_s = [mk(sp, f"wcol_s{i}", [128, nch * 128], BF16) for i in range(2)]
            rsc = mk(sp, "rsc", [128, 1024], F32)
            qstg2 = [mk(sp, f"qstg{i}", [128, 256], BF16) for i in range(2)]
            ln_ps = [mkp(sp, f"ln2_ps{i}", [128, 1024], BF16) for i in range(2)]
            pp = [mkp(sp, "pr_ps", [128, 512]), mkp(sp, "pr_ps2", [128, 512])]

            rt = [b.dma("sync", cosq_s[:], cos_q, stream="c0"),
                  b.dma("sync", sinq_s[:], sin_q, stream="c0"),
                  b.dma("sync", cosk_s[:], cos_k, stream="c0"),
                  b.dma("sync", sink_s[:], sin_k, stream="c0")]
            rope_tok = rt[-1]

            def lnT_view(ch, col):
                return ln2T(ch)[:, ts(col)]

            ps_free = [None, None]
            wprev = {}
            qst_prev = {}
            last_rope = [None]
            hc = nch // 2

            def load_wcol(w_ap, oc, par, deps):
                wsb = wcol_s[par]
                wtok = b.dma("gpsimd" if oc % 4 == 3 else "sync",
                             wsb.ap().rearrange("p (c f) -> p c f", f=128),
                             w_ap[:, ts(oc)].rearrange(
                                 "(c p) f -> c p f", p=128).transpose([1, 0, 2]),
                             deps=deps + [wprev.get(par)],
                             stream="bwg" if oc % 4 == 3 else f"bw{par}")
                return [wtok, wtok]
            for hf in range(S // 512):
                lt_done = []
                for i in range(4):
                    tile = hf * 4 + i
                    ld = b.dma("sync", xb_t[i % 2][:], xres[ts(tile), :],
                               deps=final_deps + lt_done[-2:] + proj_toks[-1:],
                               stream=f"bx{i % 2}")
                    lt_done.append(ln_into(xb_t[i % 2], ld, ln_s, ln_ps,
                                           lnT_view, i, deps=lt_done[-1:]))
                lnr = [lt_done[-1]]

                def own_view(ch):
                    v = ln2T(ch).rearrange("p (a g r) -> p a g r", g=2, r=64)
                    return v[:, :, 0]

                own0 = hf * 256
                for oc in range(nch):
                    qstg = qstg2[oc % 2]
                    wsb = wcol_s[oc % 2]
                    cts = load_wcol(wq_t, oc, oc % 2, lnr)
                    pps = pp[oc % 2]
                    mm = None
                    for i in range(nch):
                        mm = b.emit("tensor", lambda e, i=i, wsb=wsb, pps=pps: e.matmul(
                            pps[:, 0:256], wsb[:, ts(i)], own_view(i),
                            start=(i == 0), stop=(i == nch - 1)),
                            deps=[cts[i // hc], ps_free[oc % 2]] + lnr,
                            inc="s_pe" if i == nch - 1 else None)
                    wprev[oc % 2] = mm
                    csl = slice(own0, own0 + 256)
                    r1 = b.emit("vector", lambda e, pps=pps, csl=csl: e.tensor_tensor(
                        out=rsc[0:64, 0:256], in0=pps[0:64, 0:256],
                        in1=cosq_s[0:64, csl], op=ALU.mult),
                        deps=[mm, rope_tok, last_rope[0]], inc="s_dv")
                    r2 = b.emit("vector", lambda e, pps=pps, csl=csl: e.tensor_tensor(
                        out=rsc[0:64, 256:512], in0=pps[64:128, 0:256],
                        in1=sinq_s[0:64, csl], op=ALU.mult),
                        deps=[mm, last_rope[0]], inc="s_dv")
                    r3 = b.emit("vector", lambda e, qstg=qstg: e.tensor_tensor(
                        out=qstg[0:64, 0:256], in0=rsc[0:64, 0:256],
                        in1=rsc[0:64, 256:512], op=ALU.subtract),
                        deps=[r1, r2, qst_prev.get(oc % 2)], inc="s_dv")
                    r4 = b.emit("vector", lambda e, pps=pps, csl=csl: e.tensor_tensor(
                        out=rsc[0:64, 512:768], in0=pps[64:128, 0:256],
                        in1=cosq_s[64:128, csl], op=ALU.mult),
                        deps=[mm, last_rope[0]], inc="s_dv")
                    r5 = b.emit("vector", lambda e, pps=pps, csl=csl: e.tensor_tensor(
                        out=rsc[0:64, 768:1024], in0=pps[0:64, 0:256],
                        in1=sinq_s[64:128, csl], op=ALU.mult),
                        deps=[mm, last_rope[0]], inc="s_dv")
                    r6 = b.emit("vector", lambda e, qstg=qstg: e.tensor_tensor(
                        out=qstg[64:128, 0:256], in0=rsc[0:64, 512:768],
                        in1=rsc[0:64, 768:1024], op=ALU.add),
                        deps=[r4, r5, qst_prev.get(oc % 2)], inc="s_dv")
                    last_rope[0] = r6
                    ps_free[oc % 2] = r5
                    qst = b.dma("scalar", qsp[oc, :, csl], qstg[:], deps=[r3, r6],
                                stream=f"qs{oc % 2}")
                    qst_prev[oc % 2] = qst
                    proj_toks.append(qst)
                c0 = hf * 512
                csl = slice(c0, c0 + 512)
                for oc in range(kvh):
                    wsb = wcol_s[oc % 2]
                    cts = load_wcol(wk_t, oc, oc % 2, lnr)
                    pps = pp[oc % 2]
                    mm = None
                    for i in range(nch):
                        mm = b.emit("tensor", lambda e, i=i, wsb=wsb, pps=pps: e.matmul(
                            pps[:], wsb[:, ts(i)], ln2T(i),
                            start=(i == 0), stop=(i == nch - 1)),
                            deps=[cts[i // hc], ps_free[oc % 2]] + lnr,
                            inc="s_pe" if i == nch - 1 else None)
                    wprev[oc % 2] = mm
                    k1 = b.emit("vector", lambda e, pps=pps, csl=csl: e.tensor_tensor(
                        out=rsc[0:64, 0:512], in0=pps[0:64, :],
                        in1=cosk_s[0:64, csl], op=ALU.mult),
                        deps=[mm, rope_tok, last_rope[0]], inc="s_dv")
                    k2 = b.emit("vector", lambda e, pps=pps, csl=csl: e.tensor_tensor(
                        out=rsc[0:64, 512:1024], in0=pps[64:128, :],
                        in1=sink_s[0:64, csl], op=ALU.mult),
                        deps=[mm, last_rope[0]], inc="s_dv")
                    k3 = b.emit("vector", lambda e, oc=oc, csl=csl: e.tensor_tensor(
                        out=kbuf[oc][0:64, csl], in0=rsc[0:64, 0:512],
                        in1=rsc[0:64, 512:1024], op=ALU.subtract),
                        deps=[k1, k2], inc="s_dv")
                    k4 = b.emit("vector", lambda e, pps=pps, csl=csl: e.tensor_tensor(
                        out=rsc[64:128, 0:512], in0=pps[64:128, :],
                        in1=cosk_s[64:128, csl], op=ALU.mult),
                        deps=[mm, last_rope[0]], inc="s_dv")
                    k5 = b.emit("vector", lambda e, pps=pps, csl=csl: e.tensor_tensor(
                        out=rsc[64:128, 512:1024], in0=pps[0:64, :],
                        in1=sink_s[64:128, csl], op=ALU.mult),
                        deps=[mm, last_rope[0]], inc="s_dv")
                    k6 = b.emit("vector", lambda e, oc=oc, csl=csl: e.tensor_tensor(
                        out=kbuf[oc][64:128, csl], in0=rsc[64:128, 0:512],
                        in1=rsc[64:128, 512:1024], op=ALU.add),
                        deps=[k4, k5], inc="s_dv")
                    last_rope[0] = k6
                    ps_free[oc % 2] = k6
                    proj_toks += [k3, k6]
                with ExitStack() as sv:
                    v_ps = [mkp(sv, f"v_ps{hf}_{m}", [128, VW]) for m in range(4)]
                    bank_last = {m: None for m in range(4)}
                    for og in range(cfg.kvd // VW):
                        mmv = None
                        for i in range(nch):
                            wsb = wcol_s[i % 2]
                            wtok = b.dma("gpsimd" if i % 4 == 3 else "sync",
                                         wsb[:, 0:VW],
                                         wv_t[ts(i), og * VW:(og + 1) * VW],
                                         deps=lnr + [wprev.get(i % 2)],
                                         stream="bwg" if i % 4 == 3
                                         else f"bw{i % 2}")
                            for m in range(4):
                                dep_m = [wtok] + lnr
                                if i == 0:
                                    dep_m.append(bank_last[m])
                                mmv = b.emit("tensor",
                                             lambda e, i=i, m=m, wsb=wsb: e.matmul(
                                                 v_ps[m][:], ln2T(i)[:, ts(m)],
                                                 wsb[:, 0:VW],
                                                 start=(i == 0), stop=(i == nch - 1)),
                                             deps=dep_m,
                                             inc="s_pe" if m == 3 else None)
                            wprev[i % 2] = mmv
                        for m in range(4):
                            tile = hf * 4 + m
                            tk = b.emit("scalar", lambda e, m=m, og=og, tile=tile: e.copy(
                                out=vbuf[tile][:, og * VW:(og + 1) * VW],
                                in_=v_ps[m][:]), deps=[mmv], inc="s_ac")
                            bank_last[m] = tk
                            proj_toks.append(tk)
        b.barrier(proj_toks)

        attn_toks = []
        with ExitStack() as sat:
            qcur2 = [mk(sat, f"qcur{i}", [128, nch * 64], BF16) for i in range(2)]
            s_sb = [[mk(sat, f"s_sb{gp}_{p}", [128, S], F32) for p in range(2)]
                    for gp in range(2)]
            p_sb = [[mk(sat, f"p_sb{gp}_{p}", [128, S], BF16) for p in range(2)]
                    for gp in range(2)]
            pt_sb = [mk(sat, f"pt_sb{gp}", [128, (S // 128) * 256], BF16)
                     for gp in range(2)]
            cm_sb = [mk(sat, f"cm_sb{gp}", [128, S // 16], F32) for gp in range(2)]
            m8_sb = [mk(sat, f"m8_sb{gp}", [128, 8], F32) for gp in range(2)]
            ta_sb = [mk(sat, f"ta_sb{gp}", [128, S // 16], F32) for gp in range(2)]
            mx_sb = [mk(sat, f"mx_sb{gp}", [128, 128], F32) for gp in range(2)]
            rs_sb = [mk(sat, f"rs_sb{gp}", [128, 8], F32) for gp in range(2)]
            mend_s = mk(sat, "mend_s", [128, 128], F32)
            mrec_s = mk(sat, "mrec_s", [128, 128], F32)
            avst = [mk(sat, f"avst{gp}", [128, 256], BF16) for gp in range(2)]
            sps = [[mkp(sat, f"s_ps{gp}_{p}", [128, 512]) for p in range(2)]
                   for gp in range(2)]
            ptp = [mkp(sat, f"ptp{i}", [128, 1024], BF16) for i in range(2)]
            av_ps = [mkp(sat, f"av_ps{gp}", [128, 256]) for gp in range(2)]

            mtok = [b.dma("sync", mend_s[:], m_end, stream="c0"),
                    b.dma("sync", mrec_s[:], m_rec, stream="c0")]

            iters = [(j, g) for j in range(cfg.qt) for g in range(kvh)]
            if cfg.attn_iters is not None:
                iters = iters[:cfg.attn_iters]
            qload_tok = {}
            last_score_q = {}
            ps_prev = [[None, None], [None, None]]
            sm_prev = [[None, None], [None, None]]
            ptk_prev = [[None, None], [None, None]]
            av_prev = [None, None]
            avc_prev = [None, None]
            evs = [None, None]
            store_prev = None

            for it, (j, g) in enumerate(iters):
                gp = it % 2
                L = 128 * (j + 1)
                nchk = L // 16
                if g == 0:
                    qload_tok[j] = b.dma(
                        "sync", qcur2[j % 2].ap().rearrange("p (c f) -> p c f", f=64),
                        qsp[:, :, ts(j, 64)].transpose([1, 0, 2]),
                        deps=proj_toks[-1:] + [last_score_q.get(j - 2)],
                        stream=f"ql{j % 2}")
                qc = qcur2[j % 2]
                ev_pair = []
                for p in range(2):
                    pps = sps[gp][p]
                    pane = s_sb[gp][p]
                    h0 = 4 * g + 2 * p
                    ev_toks = []
                    dtile = L - 128
                    for c0 in range(0, L, 512):
                        cw = min(512, L - c0)
                        mdep = [qload_tok[j], ps_prev[gp][p]]
                        m1 = b.emit("tensor", lambda e, h0=h0, c0=c0, cw=cw, pps=pps, g=g, qc=qc: e.matmul(
                            pps[0:64, 0:cw], qc[:, ts(h0, 64)],
                            kbuf[g][:, c0:c0 + cw], start=True, stop=True,
                            tile_position=(0, 0)), deps=mdep, inc=None)
                        m2 = b.emit("tensor", lambda e, h0=h0, c0=c0, cw=cw, pps=pps, g=g, qc=qc: e.matmul(
                            pps[64:128, 0:cw], qc[:, ts(h0 + 1, 64)],
                            kbuf[g][:, c0:c0 + cw], start=True, stop=True,
                            tile_position=(0, 64)), deps=mdep, inc="s_pe")
                        last_score_q[j] = m2
                        lo, hi = c0, c0 + cw
                        ev = None
                        if dtile > lo:
                            ev = b.emit("vector", lambda e, lo=lo, m=min(dtile, hi), pps=pps, pane=pane: e.tensor_copy(
                                out=pane[:, lo:m], in_=pps[:, 0:m - lo]),
                                deps=[m2, sm_prev[gp][p]], inc="s_dv")
                        if hi > dtile:
                            ev = b.emit("vector", lambda e, lo=lo, dtile=dtile, hi=hi, pps=pps, pane=pane: e.tensor_tensor(
                                out=pane[:, dtile:hi],
                                in0=pps[:, dtile - lo:hi - lo],
                                in1=mend_s[:, 0:hi - dtile], op=ALU.add),
                                deps=[m2, mtok[0], sm_prev[gp][p]] + ([ev] if ev else []),
                                inc="s_dv")
                        ps_prev[gp][p] = ev
                        ev_toks.append(ev)
                    ev_pair.append(ev_toks)
                nm_pair = []
                for p in range(2):
                    nm, extok = _softmax_pair(cfg, b, j, L, nchk, s_sb[gp][p],
                                              p_sb[gp][p], cm_sb[gp], m8_sb[gp],
                                              ta_sb[gp], mx_sb[gp], rs_sb[gp],
                                              mrec_s, mtok, ev_pair[p],
                                              ptk_prev[gp][p])
                    sm_prev[gp][p] = extok
                    nm_pair.append(nm)
                pair_ptk = []
                for p in range(2):
                    ptk = None
                    for kt in range(j + 1):
                        dep_t = [nm_pair[p], av_prev[gp], evs[kt % 2]] + list(consts)
                        pe_t = b.emit("tensor", lambda e, kt=kt, p=p, gp=gp: e.transpose(
                            ptp[kt % 2][:, 0:128], p_sb[gp][p][:, ts(kt)],
                            ident_s[:]), deps=dep_t, inc="s_pe")
                        ptk = b.emit("scalar", lambda e, kt=kt, p=p, gp=gp: e.copy(
                            out=pt_sb[gp][:, kt * 256 + p * 128:
                                           kt * 256 + p * 128 + 128],
                            in_=ptp[kt % 2][:, 0:128]), deps=[pe_t], inc="s_ac")
                        evs[kt % 2] = ptk
                    ptk_prev[gp][p] = ptk
                    pair_ptk.append(ptk)
                mmav = None
                for kt in range(j + 1):
                    mmav = b.emit("tensor", lambda e, kt=kt, g=g, j=j, gp=gp: e.matmul(
                        av_ps[gp][:], vbuf[kt][:, ts(g)],
                        pt_sb[gp][:, kt * 256: kt * 256 + 256],
                        start=(kt == 0), stop=(kt == j)),
                        deps=pair_ptk + [avc_prev[gp]],
                        inc="s_pe" if kt == j else None)
                av_prev[gp] = mmav
                avc = b.emit("scalar", lambda e, gp=gp: e.copy(
                    out=avst[gp][:], in_=av_ps[gp][:]),
                    deps=[mmav, store_prev], inc="s_ac")
                avc_prev[gp] = avc
                store_prev = b.dma("scalar",
                                   asp[4 * g: 4 * g + 4, :, ts(j, 64)].transpose([1, 0, 2]),
                                   avst[gp].ap().rearrange("p (c f) -> p c f", f=64),
                                   deps=[avc], stream="as0")
                attn_toks.append(store_prev)
        b.barrier(attn_toks)

    op_toks = []
    with ExitStack() as so:
        ao_t = mk(so, "ao_sb", [128, nch * 512], BF16)

        def ao_sb(ch):
            return ao_t[:, ch * 512:(ch + 1) * 512]
        wos = [mk(so, f"wos{i}", [128, 512], BF16) for i in range(2)]
        wof = [mk(so, f"wof{i}", [128, 512], F32) for i in range(2)]
        xo_sb = mk(so, "xo_sb", [128, 512], F32)
        op_ps = [mkp(so, f"op_ps{m}", [128, 512]) for m in range(4)]

        bank_last = {m: None for m in range(4)}
        wofprev = {}
        for mg in range(cfg.own // 512):
            ao_tok = b.dma("sync", ao_t.ap().rearrange("p (c f) -> p c f", f=512),
                           asp[:, :, ts(mg, 512)].transpose([1, 0, 2]),
                           deps=attn_toks[-1:] + [bank_last[m] for m in range(4)],
                           stream="ao")
            wlast = {0: None, 1: None}
            for og in range(nog):
                mmo = None
                for ho in range(nch):
                    wsb = wos[ho % 2]
                    if ho % 4 == 3:
                        wtok = b.dma("gpsimd", wsb[:], wo_t[ts(ho), ts(og, 512)],
                                     deps=[ao_tok, wlast[ho % 2]], stream="wog")
                    else:
                        lt = b.dma("sync", wof[ho % 2][:],
                                   wo_t[ts(ho), ts(og, 512)],
                                   deps=[ao_tok, wofprev.get(ho % 2)],
                                   stream=f"wo{ho % 2}")
                        wtok = b.emit("scalar", lambda e, ho=ho, wsb=wsb: e.copy(
                            out=wsb[:], in_=wof[ho % 2][:]),
                            deps=[lt, wlast[ho % 2]], inc="s_ac")
                        wofprev[ho % 2] = wtok
                    for m in range(4):
                        dep_m = [wtok, ao_tok]
                        if ho == 0:
                            dep_m.append(bank_last[m])
                        mmo = b.emit("tensor", lambda e, ho=ho, m=m, wsb=wsb: e.matmul(
                            op_ps[m][:], ao_sb(ho)[:, ts(m)], wsb[:],
                            start=(ho == 0), stop=(ho == nch - 1)),
                            deps=dep_m,
                            inc="s_pe" if m == 3 else None)
                    wlast[ho % 2] = mmo
                for m in range(4):
                    i_own = mg * 4 + m
                    g0, g1 = own_rows(xres, i_own, ts(og, 512))
                    b.dma("sync", xo_sb[0:64, :], g0, deps=[mmo] + op_toks[-1:],
                          stream="xo")
                    xl = b.dma("sync", xo_sb[64:128, :], g1,
                               deps=[mmo] + op_toks[-1:], stream="xo")
                    ad = b.emit("vector", lambda e, m=m: e.tensor_tensor(
                        out=xo_sb[:], in0=xo_sb[:], in1=op_ps[m][:], op=ALU.add),
                        deps=[xl, mmo], inc="s_dv")
                    bank_last[m] = ad
                    b.dma("scalar", g0, xo_sb[0:64, :], deps=[ad], stream="xs")
                    op_toks.append(b.dma("scalar", g1, xo_sb[64:128, :], deps=[ad],
                                         stream="xs"))
    b.barrier(op_toks)


def _softmax_pair(cfg, b, j, L, nchk, s_sb, p_sb, cm_sb, m8_sb, ta_sb,
                  mx_sb, rs_sb, mrec_s, mtok, ev_toks, p_war):
    cm = b.emit("vector", lambda e, L=L, nchk=nchk: e.tensor_reduce(
        out=cm_sb[:, 0:nchk],
        in_=s_sb.ap()[:, 0:L].rearrange("p (c k) -> p c k", k=16),
        op=ALU.max, axis=AX.X), deps=ev_toks, inc="s_dv")
    m8 = b.emit("vector", lambda e, nchk=nchk: e.max(
        out=m8_sb[:], in_=cm_sb[:, 0:nchk]), deps=[cm], inc="s_dv")
    ta = b.emit("vector", lambda e, nchk=nchk: e.tensor_scalar(
        out=ta_sb[:, 0:nchk], in0=cm_sb[:, 0:nchk], scalar1=m8_sb[:, 7:8],
        scalar2=NEGV, op0=ALU.is_lt, op1=ALU.mult), deps=[m8], inc="s_dv")
    deps_f = [ta]
    if j >= 2:
        if j >= 3:
            tav = ta_sb.ap()[:, 8:8 * (j - 1)].unsqueeze(2).broadcast_to(
                [128, 8 * (j - 1) - 8, 16])
            deps_f.append(b.emit("vector", lambda e, j=j, tav=tav: e.tensor_tensor(
                out=s_sb.ap()[:, 128:128 * (j - 1)].rearrange(
                    "p (c k) -> p c k", k=16),
                in0=s_sb.ap()[:, 128:128 * (j - 1)].rearrange(
                    "p (c k) -> p c k", k=16),
                in1=tav, op=ALU.add), deps=[ta], inc="s_dv"))
        tav2 = ta_sb.ap()[:, 8 * (j - 1):8 * j].unsqueeze(2).broadcast_to(
            [128, 8, 16])
        mx = b.emit("vector", lambda e, tav2=tav2: e.tensor_tensor(
            out=mx_sb.ap().rearrange("p (c k) -> p c k", k=16),
            in0=mrec_s.ap().rearrange("p (c k) -> p c k", k=16),
            in1=tav2, op=ALU.max), deps=[ta, mtok[1]], inc="s_dv")
        deps_f.append(b.emit("vector", lambda e, j=j: e.tensor_tensor(
            out=s_sb[:, 128 * (j - 1):128 * j],
            in0=s_sb[:, 128 * (j - 1):128 * j],
            in1=mx_sb[:], op=ALU.add), deps=[mx], inc="s_dv"))
    rm = b.emit("vector", lambda e, L=L: e.tensor_reduce(
        out=rs_sb[:, 0:1], in_=s_sb[:, 0:L], op=ALU.max, axis=AX.X,
        negate=True), deps=deps_f, inc="s_dv")
    ex = b.emit("scalar", lambda e, L=L: e.activation(
        out=p_sb[:, 0:L], in_=s_sb[:, 0:L], func=AF.Exp,
        bias=rs_sb[:, 0:1], accum_out=rs_sb[:, 1:2]),
        deps=[rm, p_war], inc="s_ac")
    rc = b.emit("vector", lambda e: e.reciprocal(
        out=rs_sb[:, 2:3], in_=rs_sb[:, 1:2]), deps=[ex], inc="s_dv")
    nm = b.emit("vector", lambda e, L=L: e.tensor_scalar(
        out=p_sb[:, 0:L], in0=p_sb[:, 0:L], scalar1=rs_sb[:, 2:3], scalar2=None,
        op0=ALU.mult), deps=[rc], inc="s_dv")
    return nm, ex


def _ffn_split(cfg, b, nc, stack, mk, mkp, ln_into, consts, *, biasb, wlt,
               lnTsp, src, dst, ntiles):
    d, nch, nog = cfg.d, cfg.nch, cfg.nog
    hog = nog // 2
    cols0 = hog * 512
    x_t = [mk(stack, f"fx_t{i}", [128, d], F32) for i in range(4)]
    ln_s = mk(stack, "fln_s", [128, d], BF16)
    lnT_t = mk(stack, "flnT", [128, nch * 512], BF16)
    w_sb = [mk(stack, f"fw_sb{i}", [128, 512], BF16) for i in range(2)]
    ev_s = mk(stack, "fev_s", [128, 512], F32)
    biasb_s = mk(stack, "fbias_s", [128, d], F32)
    ffn_ps = [mkp(stack, f"f_ps{m}", [128, 512]) for m in range(4)]
    ln_ps = [mkp(stack, f"fln_ps{i}", [128, 1024], BF16) for i in range(2)]

    bt = b.dma("sync", biasb_s[:], biasb, stream="c0")
    hb = b.emit("vector", lambda e: e.tensor_scalar(
        out=biasb_s[:], in0=biasb_s[:], scalar1=0.5, scalar2=None,
        op0=ALU.mult), deps=[bt], inc="s_dv")

    def lnT(ch):
        return lnT_t[:, ch * 512:(ch + 1) * 512]

    def lnT_view(ch, col):
        return lnT(ch)[:, ts(col)]

    st = dict(prev_half=[], lnr_guard=[], bank_guard={m: None for m in range(4)})

    def og_block(og, lnr, wlast, bank_last, deq_blk, res_toks):
        wdep = deq_blk[min(4 * og + 3, len(deq_blk) - 1)] if deq_blk else None
        mm_last = None
        for i in range(nch):
            wsb = w_sb[i % 2]
            wl_view = wlt[4 * og: 4 * og + 4, ts(i), :].transpose([1, 0, 2])
            wtok = b.dma("sync", wsb.ap().rearrange("p (bb f) -> p bb f", f=128),
                         wl_view, deps=[wdep, wlast[i % 2]] + lnr,
                         stream=f"fw{i % 2}")
            for m in range(4):
                dep_m = [wtok] + lnr
                if i == 0:
                    dep_m.append(bank_last[m] if bank_last[m] is not None
                                 else st["bank_guard"][m])
                mm_last = b.emit("tensor", lambda e, i=i, m=m, wsb=wsb: e.matmul(
                    ffn_ps[m][:], lnT(i)[:, ts(m)], wsb[:],
                    start=(i == 0), stop=(i == nch - 1)),
                    deps=dep_m, inc="s_pe" if m == 3 else None)
            wlast[i % 2] = mm_last
        for m in range(4):
            e1 = b.emit("vector", lambda e, m=m, og=og: e.scalar_tensor_tensor(
                out=ev_s[:], in0=ffn_ps[m][:], scalar=0.5,
                in1=biasb_s[:, ts(og, 512)], op0=ALU.mult, op1=ALU.add),
                deps=[mm_last, hb] + res_toks[-1:], inc="s_dv")
            e2 = b.emit("gpsimd", lambda e, m=m, og=og: e.tensor_tensor(
                out=x_t[m][:, ts(og, 512)], in0=x_t[m][:, ts(og, 512)],
                in1=ev_s[:], op=ALU.add), deps=[e1], inc="s_pl")
            bank_last[m] = e2
            res_toks.append(e2)
        return mm_last

    def pass0(deq_blk, after_og=None):
        slot = 0
        for hf in range(ntiles // 4):
            ln_toks = []
            res_toks = []
            for i in range(4):
                ld = b.dma("sync", x_t[i][:], src(hf * 4 + i),
                           deps=st["prev_half"] + ln_toks[-1:],
                           stream=f"fx{i % 2}")
                ln_toks.append(ln_into(x_t[i], ld, ln_s, ln_ps, lnT_view, i,
                                       deps=ln_toks[-1:] + st["lnr_guard"]))
            lnr = [ln_toks[-1]]
            wlast = {0: None, 1: None}
            bank_last = {m: None for m in range(4)}
            for og in range(hog):
                og_block(og, lnr, wlast, bank_last, deq_blk, res_toks)
                if after_og is not None:
                    after_og(slot)
                slot += 1
            sp = b.dma("scalar", lnTsp[hf], lnT_t[:], deps=lnr, stream="fsp")
            st["lnr_guard"] = [wlast[0], wlast[1], sp, res_toks[-1]]
            st["bank_guard"] = dict(bank_last)
            sts = []
            for i in range(4):
                sts.append(b.dma("scalar", dst(hf * 4 + i)[:, 0:cols0],
                                 x_t[i][:, 0:cols0], deps=res_toks,
                                 stream=f"fs{i % 2}"))
            st["prev_half"] = sts

    def pass1(deq_blk):
        all_st = []
        for hf in range(ntiles // 4):
            rl = b.dma("sync", lnT_t[:], lnTsp[hf], deps=st["lnr_guard"],
                       stream="frl")
            lnr = [rl]
            ld_last = [None, None]
            for i in range(4):
                ld_last[i % 2] = b.dma("sync", x_t[i][:, cols0:d],
                                       src(hf * 4 + i)[:, cols0:d],
                                       deps=st["prev_half"], stream=f"fx{i % 2}")
            lnr += ld_last
            res_toks = []
            wlast = {0: None, 1: None}
            bank_last = {m: None for m in range(4)}
            for og in range(hog, nog):
                og_block(og, lnr, wlast, bank_last, deq_blk, res_toks)
            st["lnr_guard"] = [wlast[0], wlast[1], res_toks[-1]]
            st["bank_guard"] = dict(bank_last)
            sts = []
            for i in range(4):
                sts.append(b.dma("scalar", dst(hf * 4 + i)[:, cols0:d],
                                 x_t[i][:, cols0:d], deps=res_toks,
                                 stream=f"fs{i % 2}"))
            st["prev_half"] = sts
            all_st += sts
        return all_st

    return pass0, pass1


def _ffn_sublayer(cfg, b, nc, stack, mk, mkp, ln_into, consts, biasb, wlt,
                  deq_blk, src_tile, dst_tile, ntiles, pre="f"):
    d, nch, nog = cfg.d, cfg.nch, cfg.nog
    x_t = [mk(stack, pre + f"x_t{i}", [128, d], F32) for i in range(4)]
    ln_s = mk(stack, pre + "ln_s", [128, d], BF16)
    lnT_t = mk(stack, pre + "lnT", [128, nch * 512], BF16)

    def lnT(ch):
        return lnT_t[:, ch * 512:(ch + 1) * 512]
    w_sb = [mk(stack, pre + f"w_sb{i}", [128, 512], BF16) for i in range(2)]
    ev_s = mk(stack, pre + "ev_s", [128, 512], F32)
    biasb_s = mk(stack, pre + "bias_s", [128, d], F32)
    ffn_ps = [mkp(stack, pre + f"_ps{m}", [128, 512]) for m in range(4)]
    ln_ps = [mkp(stack, pre + f"ln_ps{i}", [128, 1024], BF16) for i in range(2)]

    bt = b.dma("sync", biasb_s[:], biasb, stream="c0")
    hb = b.emit("vector", lambda e: e.tensor_scalar(
        out=biasb_s[:], in0=biasb_s[:], scalar1=0.5, scalar2=None,
        op0=ALU.mult), deps=[bt], inc="s_dv")

    def lnT_view(ch, col):
        return lnT(ch)[:, ts(col)]

    all_st = []
    prev_half = []
    for hf in range(ntiles // 4):
        ln_toks = []
        ld_toks = []
        for i in range(4):
            stile = src_tile(hf * 4 + i)
            if isinstance(stile, list):
                b.dma("sync", x_t[i][0:64, :], stile[0],
                      deps=prev_half + ln_toks[-1:], stream=pre + f"x{i}")
                ld = b.dma("sync", x_t[i][64:128, :], stile[1],
                           deps=prev_half + ln_toks[-1:], stream=pre + f"x{i}")
            else:
                ld = b.dma("sync", x_t[i][:], stile,
                           deps=prev_half + ln_toks[-1:], stream=pre + f"x{i}")
            ld_toks.append(ld)
            ln_toks.append(ln_into(x_t[i], ld, ln_s, ln_ps, lnT_view, i,
                                   deps=ln_toks[-1:]))
        lnr = [ln_toks[-1]]
        res_toks = []
        bank_last = {m: None for m in range(4)}
        wlast = {0: None, 1: None}
        for og in range(nog):
            wdep = deq_blk[min(4 * og + 3, len(deq_blk) - 1)] if deq_blk else None
            mm_last = None
            for i in range(nch):
                wsb = w_sb[i % 2]
                wl_view = wlt[4 * og: 4 * og + 4, ts(i), :].transpose([1, 0, 2])
                wtok = b.dma("sync", wsb.ap().rearrange("p (b f) -> p b f", f=128),
                             wl_view,
                             deps=[wdep, wlast[i % 2]] + lnr, stream=pre + f"w{i % 2}")
                for m in range(4):
                    dep_m = [wtok] + lnr
                    if i == 0 and bank_last[m] is not None:
                        dep_m.append(bank_last[m])
                    mm_last = b.emit("tensor", lambda e, i=i, m=m, wsb=wsb: e.matmul(
                        ffn_ps[m][:], lnT(i)[:, ts(m)], wsb[:],
                        start=(i == 0), stop=(i == nch - 1)),
                        deps=dep_m,
                        inc="s_pe" if m == 3 else None)
                wlast[i % 2] = mm_last
            for m in range(4):
                e1 = b.emit("vector", lambda e, m=m, og=og: e.scalar_tensor_tensor(
                    out=ev_s[:], in0=ffn_ps[m][:], scalar=0.5,
                    in1=biasb_s[:, ts(og, 512)], op0=ALU.mult, op1=ALU.add),
                    deps=[mm_last, hb] + res_toks[-1:], inc="s_dv")
                e2 = b.emit("gpsimd", lambda e, m=m, og=og: e.tensor_tensor(
                    out=x_t[m][:, ts(og, 512)], in0=x_t[m][:, ts(og, 512)],
                    in1=ev_s[:], op=ALU.add), deps=[e1], inc="s_pl")
                bank_last[m] = e2
                res_toks.append(e2)
        st_toks = []
        for i in range(4):
            dtile = dst_tile(hf * 4 + i)
            if isinstance(dtile, list):
                b.dma("scalar", dtile[0], x_t[i][0:64, :], deps=res_toks,
                      stream=pre + f"s{i % 2}")
                st_toks.append(b.dma("scalar", dtile[1], x_t[i][64:128, :],
                                     deps=res_toks, stream=pre + f"s{i % 2}"))
            else:
                st_toks.append(b.dma("scalar", dtile, x_t[i][:], deps=res_toks,
                                     stream=pre + f"s{i % 2}"))
        all_st += st_toks
        prev_half = st_toks
    return all_st


def shard_inputs(cfg, x_full, qweight, lut4, bias, wq, wk, wv, wo):
    wq_t = np.ascontiguousarray(wq.T)
    wk_t = np.ascontiguousarray(wk.T)
    wv_t = np.ascontiguousarray(wv.T)
    wo_t = np.ascontiguousarray(wo.T)
    qw4 = np.ascontiguousarray(qweight[0:4])
    biasb = np.broadcast_to(bias[None, :], (128, cfg.d)).copy()
    in_maps = []
    for c in range(8):
        bb, half = c // 2, c % 2
        t = make_tables(cfg, half)
        in_maps.append(dict(
            xin=permute_seq(x_full[bb], half),
            qw4=qw4, lut4=np.ascontiguousarray(lut4), biasb=biasb,
            wq_t=wq_t, wk_t=wk_t, wv_t=wv_t, wo_t=wo_t,
            mask32=t["mask32"], ident=t["ident"],
            cos_k=t["cos_k"], sin_k=t["sin_k"],
            cos_q=t["cos_q"], sin_q=t["sin_q"],
            m_end=t["m_end"], m_rec=t["m_rec"]))
    return in_maps


def assemble_output(cfg, results):
    out = np.zeros((4, cfg.seq, cfg.d), dtype=np.float32)
    o = np.arange(cfg.own)
    for c in range(8):
        bb, half = c // 2, c % 2
        rows = np.asarray(results[c]["xout"])
        out[bb, 128 * (o // 64) + 64 * half + (o % 64), :] = rows
    return out


def kernel(L0_attn_hidden_states, L0_attn_wq, L0_attn_wk, L0_attn_wv, L0_attn_wo,
           L0_ffn_qweight, L0_ffn_lut4, L0_ffn_lut8, L0_ffn_bias):
    cfg = Cfg()
    nc = build_nc(cfg)
    in_maps = shard_inputs(cfg, np.asarray(L0_attn_hidden_states),
                           np.asarray(L0_ffn_qweight), np.asarray(L0_ffn_lut4),
                           np.asarray(L0_ffn_bias), np.asarray(L0_attn_wq),
                           np.asarray(L0_attn_wk), np.asarray(L0_attn_wv),
                           np.asarray(L0_attn_wo))
    res = run_bass_kernel_spmd(nc, in_maps, core_ids=list(range(8)))
    return assemble_output(cfg, res.results)
